# version 34
# baseline (speedup 1.0000x reference)
import contextlib
import numpy as np
import ml_dtypes
import concourse.bass as bass
import concourse.mybir as mybir
from concourse.bass_utils import run_bass_kernel_spmd

F32 = mybir.dt.float32
BF16 = mybir.dt.bfloat16
AF = mybir.ActivationFunctionType
ALU = mybir.AluOpType
AX = mybir.AxisListType

S = 8192
D = 1024
NT = 64
OWN = 2048
ONT = 16
NPASS = 4
HPP = 4
GK = 3
NG = 22
EPS = 1e-5

ENGS = ("sync", "gpsimd", "vector", "tensor", "scalar")


class _Prog:

    def __init__(self, nc, es):
        self.nc = nc
        self.es = es
        self.ops = {e: [] for e in ENGS}
        self.sem = {}
        self.cnt = {}

    def new_sem(self, name):
        self.sem[name] = self.es.enter_context(self.nc.semaphore(name))
        self.cnt[name] = 0

    def W(self, eng, name, val=None):
        v = self.cnt[name] if val is None else val
        s = self.sem[name]
        self.ops[eng].append(lambda E, s=s, v=v: E.wait_ge(s, v))

    def do(self, eng, fn, inc=None, n=1):
        if inc is not None:
            self.cnt[inc] += n
            s = self.sem[inc]
            self.ops[eng].append(lambda E, fn=fn, s=s, n=n: fn(E).then_inc(s, n))
        else:
            self.ops[eng].append(fn)

    def selfw(self, eng, name):
        self.W(eng, name)

    def bump(self, eng, name, n=1):
        self.cnt[name] += n
        s = self.sem[name]
        self.ops[eng].append(lambda E, s=s, n=n: E.sem_inc(s, n))


def _ins1(ap, axis):
    pat = [list(d) for d in ap.ap]
    pat = pat[:axis] + [[0, 1]] + pat[axis:]
    return bass.AP(ap.tensor, ap.offset, pat)


def _build():
    nc = bass.Bass("TRN2", target_bir_lowering=False, debug=False)

    X = nc.dram_tensor("X", [S, D], F32, kind="ExternalInput")
    Win = nc.dram_tensor("Win", [D, 3 * D], F32, kind="ExternalInput")
    BinRow = nc.dram_tensor("BinRow", [1, 3 * D], F32, kind="ExternalInput")
    Wout = nc.dram_tensor("Wout", [D, D], F32, kind="ExternalInput")
    BoutCol = nc.dram_tensor("BoutCol", [128, 8], F32, kind="ExternalInput")
    BoutRow = nc.dram_tensor("BoutRow", [1, D], F32, kind="ExternalInput")
    GamCol = nc.dram_tensor("GamCol", [128, 8], F32, kind="ExternalInput")
    BetCol = nc.dram_tensor("BetCol", [128, 8], F32, kind="ExternalInput")
    Wffn = nc.dram_tensor("Wffn", [D, D], F32, kind="ExternalInput")
    RopeC = nc.dram_tensor("RopeC", [S, 32], F32, kind="ExternalInput")
    RopeS = nc.dram_tensor("RopeS", [S, 32], F32, kind="ExternalInput")
    OUT = nc.dram_tensor("OUT", [OWN, D], F32, kind="ExternalOutput")

    zT_h = nc.dram_tensor("zT_h", [D, S], BF16, kind="Internal")
    wq_h = nc.dram_tensor("wq_h", [D, 3 * D], BF16, kind="Internal")
    wo_h = nc.dram_tensor("wo_h", [D, D], BF16, kind="Internal")
    wf_h = nc.dram_tensor("wf_h", [D, D], BF16, kind="Internal")

    es = contextlib.ExitStack()
    P = _Prog(nc, es)
    for s in ("lds", "lda", "ldt", "gdma", "stz", "sto", "vs", "ts", "asc", "gsc", "bar",
              "lz0", "lz1", "lz2", "sz0", "sz1", "sz2"):
        P.new_sem(s)

    sb = lambda name, shp, dt: es.enter_context(nc.sbuf_tensor(name, shp, dt))

    ctx_store = sb("ctx_store", [128, 8 * OWN], BF16)
    ropeCb = sb("ropeCb", [128, NT * 32], BF16)
    ropeSb = sb("ropeSb", [128, NT * 32], BF16)
    brow_b = sb("brow_b", [1, 3 * D], BF16)
    ident32 = sb("ident32", [128, 128], F32)
    identb = sb("identb", [128, 128], BF16)
    ones32 = sb("ones32", [1, 128], F32)
    onesb = sb("onesb", [1, 128], BF16)
    epst = sb("epst", [128, 1], F32)
    gam = sb("gam", [128, 8], F32)
    bet = sb("bet", [128, 8], F32)

    barN = [0]

    def barrier():
        for e in ENGS:
            P.bump(e, "bar")
        barN[0] += 1
        for e in ENGS:
            P.W(e, "bar", 5 * barN[0])

    def quiesce():
        P.W("sync", "lds")
        P.W("sync", "ldt")
        P.W("sync", "stz")
        P.W("sync", "sto")
        P.W("scalar", "lda")
        P.W("scalar", "asc")
        P.W("gpsimd", "gdma")
        P.W("gpsimd", "gsc")
        P.W("vector", "vs")
        P.W("tensor", "ts")

    with contextlib.ExitStack() as es0:
        sb0 = lambda name, shp, dt: es0.enter_context(nc.sbuf_tensor(name, shp, dt))
        wtmp32 = [sb0(f"wtmp32_{i}", [128, 3 * D], F32) for i in range(2)]
        wtmpb = [sb0(f"wtmpb{i}", [128, 3 * D], BF16) for i in range(2)]
        wqb = [sb0(f"wqb{i}", [128, 3 * D], BF16) for i in range(2)]
        woutb = [sb0(f"woutb{i}", [128, D], BF16) for i in range(2)]
        wot32 = [sb0(f"wot32_{i}", [128, D], F32) for i in range(2)]
        wft32 = [sb0(f"wft32_{i}", [128, D], F32) for i in range(2)]
        wtm32b = [sb0(f"wtm32b{i}", [128, 3 * D], F32) for i in range(2)]
        wsignb = [sb0(f"wsignb{i}", [128, D], BF16) for i in range(2)]
        wtr = sb0("wtr", [128, D], BF16)
        binrow = sb0("binrow", [1, 3 * D], F32)
        betb = sb0("betb", [128, 8], BF16)
        brow_ps = es0.enter_context(nc.psum_tensor("brow_ps", [1, 3 * D], F32))

        P.do("sync", lambda E: E.dma_start(out=gam[:, :], in_=GamCol[:, :]), inc="lds", n=16)
        P.do("sync", lambda E: E.dma_start(out=bet[:, :], in_=BetCol[:, :]), inc="lds", n=16)
        P.do("sync", lambda E: E.dma_start(out=binrow[:, :], in_=BinRow[:, :]), inc="lds", n=16)
        P.do("gpsimd", lambda E: E.dma_start(
            out=ropeCb.rearrange("p (t c) -> p t c", c=32),
            in_=RopeC.rearrange("(t p) c -> p t c", p=128)), inc="gdma", n=16)
        P.do("gpsimd", lambda E: E.dma_start(
            out=ropeSb.rearrange("p (t c) -> p t c", c=32),
            in_=RopeS.rearrange("(t p) c -> p t c", p=128)), inc="gdma", n=16)
        P.do("gpsimd", lambda E: E.memset(ident32[:, :], 0.0))
        P.do("gpsimd", lambda E: E.affine_select(
            out=ident32[:, :], in_=ident32[:, :], compare_op=ALU.not_equal,
            fill=1.0, base=0, pattern=[[-1, 128]], channel_multiplier=1))
        P.do("gpsimd", lambda E: E.memset(ones32[:, :], 1.0), inc="gsc")
        P.W("vector", "gsc")
        P.do("vector", lambda E: E.tensor_copy(identb[:, :], ident32[:, :]), inc="vs")
        P.do("gpsimd", lambda E: E.memset(onesb[:, :], 1.0))
        P.do("gpsimd", lambda E: E.memset(epst[:, :], EPS), inc="gsc")

        P.W("vector", "lds", 32)
        P.do("vector", lambda E: E.tensor_copy(betb[:, :], bet[:, :]), inc="vs")
        vs_betb = P.cnt["vs"]

        vs_scale = {}
        ts_brow = {}
        win_ld = {}
        wbuf = lambda i: wtmp32[(i // 2) % 2] if i % 2 == 0 else wtm32b[(i // 2) % 2]
        for i in range(8):
            ring, lsem = ("sync", "lds") if i % 2 == 0 else ("scalar", "lda")
            if i >= 4:
                P.W(ring, "vs", vs_scale[i - 4])
            P.do(ring, lambda E, i=i: E.dma_start(
                out=wbuf(i)[:, :], in_=Win[i * 128:(i + 1) * 128, :]), inc=lsem, n=16)
            win_ld[i] = P.cnt[lsem]
            P.W("vector", lsem, win_ld[i])
            if i >= 2:
                P.W("vector", "ts", ts_brow[i - 2])
            P.do("vector", lambda E, i=i: E.tensor_copy(wtmpb[i % 2][:, :], wbuf(i)[:, :]), inc="vs")
            P.W("tensor", "vs")
            P.W("tensor", "vs", vs_betb)
            for j in range(6):
                P.do("tensor", lambda E, i=i, j=j: E.matmul(
                    brow_ps[0:1, j * 512:(j + 1) * 512], betb[:, i:i + 1],
                    wtmpb[i % 2][:, j * 512:(j + 1) * 512],
                    start=(i == 0), stop=(i == 7)), inc=("ts" if j == 5 else None))
            ts_brow[i] = P.cnt["ts"]
            if i >= 2:
                P.W("vector", "stz")
            P.do("vector", lambda E, i=i: E.tensor_scalar(
                wqb[i % 2][:, :], wbuf(i)[:, :], gam[:, i:i + 1], None, ALU.mult), inc="vs")
            vs_scale[i] = P.cnt["vs"]
            P.W("sync", "vs", vs_scale[i])
            P.do("sync", lambda E, i=i: E.dma_start(
                out=wq_h[i * 128:(i + 1) * 128, :], in_=wqb[i % 2][:, :]), inc="stz", n=16)
        P.W("vector", "ts")
        P.W("vector", "lds", 48)
        P.do("vector", lambda E: E.tensor_tensor(
            brow_b[:, :], brow_ps[0:1, :], binrow[:, :], ALU.add), inc="vs")

        vs_wout = {}
        lda_wout = {}
        for i in range(8):
            if i >= 2:
                P.W("scalar", "vs", vs_wout[i - 2])
            P.do("scalar", lambda E, i=i: E.dma_start(
                out=wot32[i % 2][:, :], in_=Wout[i * 128:(i + 1) * 128, :]), inc="lda", n=16)
            lda_wout[i] = P.cnt["lda"]
            P.W("vector", "lda", lda_wout[i])
            if i >= 2:
                P.W("vector", "stz")
            P.do("vector", lambda E, i=i: E.tensor_copy(woutb[i % 2][:, :], wot32[i % 2][:, :]), inc="vs")
            vs_wout[i] = P.cnt["vs"]
            P.W("sync", "vs", vs_wout[i])
            P.do("sync", lambda E, i=i: E.dma_start(
                out=wo_h[i * 128:(i + 1) * 128, :], in_=woutb[i % 2][:, :]), inc="stz", n=16)

        ldt_ffn = {}
        for oc in range(8):
            P.do("scalar", lambda E, oc=oc: E.dma_start(
                out=wft32[oc % 2][:, :], in_=Wffn[oc * 128:(oc + 1) * 128, :]), inc="lda", n=16)
            P.W("scalar", "lda")
            if oc >= 2:
                P.W("scalar", "ldt", ldt_ffn[oc - 2])
            P.do("scalar", lambda E, oc=oc: E.sign(wsignb[oc % 2][:, :], wft32[oc % 2][:, :]), inc="asc")
            P.W("sync", "asc")
            P.W("sync", "stz")
            for j in range(8):
                P.do("sync", lambda E, j=j, oc=oc: E.dma_start(
                    out=wtr[:, j * 128:(j + 1) * 128],
                    in_=wsignb[oc % 2][:, j * 128:(j + 1) * 128], transpose=True),
                    inc="ldt", n=16)
            ldt_ffn[oc] = P.cnt["ldt"]
            P.W("sync", "ldt")
            P.do("sync", lambda E, oc=oc: E.dma_start(
                out=wf_h.rearrange("(j p) o -> p j o", p=128)[:, :, oc * 128:(oc + 1) * 128],
                in_=wtr.rearrange("p (j q) -> p j q", q=128)), inc="stz", n=16)

        sb1 = sb0
        es1 = es0
        NZ = 3
        xt = [sb1(f"xt{i}", [128, D], F32) for i in range(NZ)]
        zb = [sb1(f"zb{i}", [128, D], BF16) for i in range(NZ)]
        ztmp = [sb1(f"ztmp{i}", [128, D], BF16) for i in range(NZ)]
        bns = [sb1(f"bns{i}", [128, 12], F32) for i in range(NZ)]
        stats = [sb1(f"stats{i}", [128, 2], F32) for i in range(NZ)]
        lnt = [sb1(f"lnt{i}", [128, 1], F32) for i in range(NZ)]
        rstd = [sb1(f"rstd{i}", [128, 1], F32) for i in range(NZ)]
        negmr = [sb1(f"negmr{i}", [128, 1], F32) for i in range(NZ)]
        ztp = [es1.enter_context(nc.psum_tensor(f"ztp{i}", [128, 1024], BF16)) for i in range(2)]

        lds0 = P.cnt["lds"]
        vs_done = {}
        asc_done = {}
        ts_done = {}
        vse_done = {}
        stz_done = {}
        lzs = [P.sem["lz0"], P.sem["lz1"], P.sem["lz2"]]
        szs = [P.sem["sz0"], P.sem["sz1"], P.sem["sz2"]]
        lzc = {0: 0, 1: 0, 2: 0}
        szc = {0: 0, 1: 0, 2: 0}
        vs_done = {}
        asc_done = {}
        ts_done = {}
        vse_done = {}
        for t0_ in range(NZ):
            P.do("gpsimd", lambda E, t=t0_: E.dma_start(
                out=xt[t % NZ][:, :], in_=X[t * 128:(t + 1) * 128, :]).then_inc(
                lzs[t % NZ], 16) or None)
            lzc[t0_ % NZ] += 16
        for t in range(NT):
            bz = t % NZ
            self_lz = lzc[bz]
            P.ops["vector"].append(lambda E, s=lzs[bz], v=self_lz: E.wait_ge(s, v))
            P.do("vector", lambda E, bz=bz: E.bn_stats(bns[bz][:, 0:6], xt[bz][:, 0:512]))
            P.do("vector", lambda E, bz=bz: E.bn_stats(bns[bz][:, 6:12], xt[bz][:, 512:1024]), inc="vs")
            P.selfw("vector", "vs")
            if t >= NZ:
                P.W("vector", "asc", asc_done[t - NZ])
            P.do("vector", lambda E, bz=bz: E.bn_aggr(stats[bz][:, :], bns[bz][:, :]), inc="vs")
            vs_aggr = P.cnt["vs"]

            P.W("scalar", "vs", vs_aggr)
            P.do("scalar", lambda E, bz=bz: E.activation(
                lnt[bz][:, :], stats[bz][:, 1:2], AF.Ln, bias=epst[:, 0:1]), inc="asc")
            P.selfw("scalar", "asc")
            P.do("scalar", lambda E, bz=bz: E.activation(
                rstd[bz][:, :], lnt[bz][:, :], AF.Exp, scale=-0.5), inc="asc")
            asc_rstd = P.cnt["asc"]

            P.W("vector", "asc", asc_rstd)
            P.do("vector", lambda E, bz=bz: E.tensor_scalar(
                negmr[bz][:, :], stats[bz][:, 0:1], rstd[bz][:, 0:1], -1.0, ALU.mult, ALU.mult),
                inc="vs")
            vs_done[t] = P.cnt["vs"]

            P.W("scalar", "vs", vs_done[t])
            if t >= 2:
                P.W("scalar", "ts", ts_done[t - 2])
            P.do("scalar", lambda E, bz=bz: E.activation(
                zb[bz % 2 if False else bz][:, :], xt[bz][:, :], AF.Identity,
                bias=negmr[bz][:, 0:1], scale=rstd[bz][:, 0:1]), inc="asc")
            asc_done[t] = P.cnt["asc"]

            P.W("tensor", "asc", asc_done[t])
            if t >= 2:
                P.W("tensor", "vs", vse_done[t - 2])
            for j in range(8):
                P.do("tensor", lambda E, bz=bz, j=j: E.transpose(
                    ztp[t % 2 if False else (bz % 2)][:, j * 128:(j + 1) * 128],
                    zb[bz][:, j * 128:(j + 1) * 128], identb[:, :]),
                    inc=("ts" if j == 7 else None))
            ts_done[t] = P.cnt["ts"]

            P.W("vector", "ts", ts_done[t])
            if t >= NZ:
                v = szc[bz]
                P.ops["vector"].append(lambda E, s=szs[bz], v=v: E.wait_ge(s, v))
            P.do("vector", lambda E, bz=bz: E.tensor_copy(
                ztmp[bz][:, :], ztp[bz % 2][:, :]), inc="vs")
            vse_done[t] = P.cnt["vs"]

            P.W("gpsimd", "vs", vse_done[t])
            P.do("gpsimd", lambda E, t=t, bz=bz: E.dma_start(
                out=zT_h.rearrange("(j p) s -> p j s", p=128)[:, :, t * 128:(t + 1) * 128],
                in_=ztmp[bz].rearrange("p (j q) -> p j q", q=128)).then_inc(
                szs[bz], 16) or None)
            szc[bz] += 16
            if t + NZ < NT:
                P.W("gpsimd", "vs", vs_done[t])
                P.W("gpsimd", "asc", asc_done[t])
                P.do("gpsimd", lambda E, t=t, bz=bz: E.dma_start(
                    out=xt[bz][:, :], in_=X[(t + NZ) * 128:(t + NZ + 1) * 128, :]).then_inc(
                    lzs[bz], 16) or None)
                lzc[bz] += 16
        z_sync_vs = P.cnt["vs"]
        z_szc = dict(szc)
        for nm, cnt in (("lz0", lzc[0]), ("lz1", lzc[1]), ("lz2", lzc[2]),
                        ("sz0", szc[0]), ("sz1", szc[1]), ("sz2", szc[2])):
            P.W("gpsimd", nm, cnt)
        quiesce()
        barrier()

    v_h = nc.dram_tensor("v_h", [128, NT * 260], BF16, kind="Internal")
    with contextlib.ExitStack() as es2:
        sb2 = lambda name, shp, dt: es2.enter_context(nc.sbuf_tensor(name, shp, dt))
        KTs = [[sb2(f"KT{s}_{i}", [128, S], BF16) for i in range(2)] for s in range(2)]
        QTs = [[sb2(f"QT{s}_{i}", [128, OWN], BF16) for i in range(2)] for s in range(2)]
        Vt = sb2("Vt", [128, NT * 260], BF16)
        vstg = [sb2(f"vstg{i}", [128, 260], BF16) for i in range(2)]
        wq_s = sb2("wq_s", [128, 8 * 768], BF16)
        zt_s = [sb2(f"zt_s{i}", [128, 8 * 128], BF16) for i in range(2)]
        PAB = [sb2(f"PAB{i}", [128, 1024], BF16) for i in range(5)]
        pairB2 = [sb2(f"pairB{s}", [128, 2], F32) for s in range(2)]
        krm = [sb2(f"krm{i}", [128, 256], BF16) for i in range(3)]
        qrm = [sb2(f"qrm{i}", [128, 256], BF16) for i in range(3)]
        rotb = sb2("rotb", [128, 256], BF16)
        tmpb = sb2("tmpb", [128, 256], BF16)
        sqg = [sb2(f"sqg{i}", [128, 512], BF16) for i in range(2)]
        normK2 = sb2("normK2", [128, 4 * NT], F32)
        normQ2 = sb2("normQ2", [128, 4 * ONT], F32)
        mk2 = sb2("mk2", [128, 4], F32)
        mq2 = sb2("mq2", [128, 4], F32)
        m1c = sb2("m1c", [4, 1], F32)
        m2c = sb2("m2c", [4, 1], F32)
        mmv = sb2("mmv", [4, 1], F32)
        negb = sb2("negb", [4, 1], F32)
        negbrow = sb2("negbrow", [1, 4], F32)
        expBias = sb2("expBias", [128, 4], F32)
        stgE = sb2("stgE", [64, 512], BF16)
        stgO = sb2("stgO", [64, 512], BF16)
        stgOn = sb2("stgOn", [64, 512], BF16)
        recEf = sb2("recEf", [1, 512], F32)
        recEb = sb2("recEb", [1, 512], BF16)
        recOf = sb2("recOf", [1, 512], F32)
        recOb = sb2("recOb", [1, 512], BF16)
        lnscr = sb2("lnscr", [1, 1024], F32)

        P.do("gpsimd", lambda E: E.memset(
            Vt.rearrange("p (t h x) -> p t h x", h=4, x=65)[:, :, :, 64:65], 1.0))
        for i in range(2):
            P.do("gpsimd", lambda E, i=i: E.memset(
                vstg[i].rearrange("p (h x) -> p h x", x=65)[:, :, 64:65], 1.0),
                inc=("gsc" if i == 1 else None))

        a_wq_guard = [0]
        vh_load_guard = [0]

        def gen_A(pp, psA, psT, direct_v):
            KT = KTs[pp % 2]
            QT = QTs[pp % 2]
            pairB = pairB2[pp % 2]
            qc, kc_, vc_ = 256 * pp, D + 256 * pp, 2 * D + 256 * pp
            P.W("sync", "ts", a_wq_guard[0])
            P.W("sync", "stz")
            for dst, srcc in ((0, kc_), (256, qc), (512, vc_)):
                P.do("sync", lambda E, dst=dst, srcc=srcc: E.dma_start(
                    out=wq_s.rearrange("p (i c) -> p i c", c=768)[:, :, dst:dst + 256],
                    in_=wq_h.rearrange("(i p) c -> p i c", p=128)[:, :, srcc:srcc + 256]),
                    inc="lds", n=16)
            lds_wq = P.cnt["lds"]
            P.do("sync", lambda E: E.dma_start(
                out=zt_s[0][:, :],
                in_=zT_h.rearrange("(i p) s -> p i s", p=128)[:, :, 0:128]),
                inc="lds", n=16)
            lds_zt_of = {0: P.cnt["lds"]}
            ts_mm = {}
            evac_done = {}
            last_evac = [0]
            rope_done = {}
            tp_ts = {}
            ptcopy = {}
            sq_done = {}
            nred = {}
            vdma = [P.cnt["gdma"]]

            def rope(buf, t):
                bv = buf
                src_o = bv.rearrange("p (h d) -> p h d", d=64)[:, :, 0:32] \
                    .rearrange("p h (i two) -> p h i two", two=2)
                rot_v = rotb.rearrange("p (h d) -> p h d", d=64)[:, :, 0:32] \
                    .rearrange("p h (i two) -> p h i two", two=2)
                P.do("vector", lambda E, s=src_o, r=rot_v: E.tensor_copy(
                    r[:, :, :, 0:1], s[:, :, :, 1:2]))
                P.do("vector", lambda E, s=src_o, r=rot_v: E.tensor_copy(
                    r[:, :, :, 1:2], s[:, :, :, 0:1]), inc="vs")
                P.selfw("vector", "vs")
                cosT = ropeCb.rearrange("p (tt c) -> p tt c", c=32)[:, t, :]
                sinT = ropeSb.rearrange("p (tt c) -> p tt c", c=32)[:, t, :]
                cosB = bass.AP(cosT.tensor, cosT.offset,
                               [list(cosT.ap[0]), [0, 4], list(cosT.ap[1])])
                sinB = bass.AP(sinT.tensor, sinT.offset,
                               [list(sinT.ap[0]), [0, 4], list(sinT.ap[1])])
                reg = bv.rearrange("p (h d) -> p h d", d=64)[:, :, 0:32]
                rreg = rotb.rearrange("p (h d) -> p h d", d=64)[:, :, 0:32]
                treg = tmpb.rearrange("p (h d) -> p h d", d=64)[:, :, 0:32]
                P.do("vector", lambda E, rr=rreg, s=sinB, tr=treg: E.tensor_tensor(
                    tr, rr, s, ALU.mult))
                P.do("vector", lambda E, rg=reg, c=cosB: E.tensor_tensor(
                    rg, rg, c, ALU.mult), inc="vs")
                P.selfw("vector", "vs")
                P.do("vector", lambda E, rg=reg, tr=treg: E.tensor_tensor(
                    rg, rg, tr, ALU.add), inc="vs")

            def emit_tp(tt):
                own_tt = tt < ONT
                P.W("tensor", "vs", rope_done[tt])
                if tt >= 1:
                    P.W("tensor", "vs", ptcopy.get(tt - 1, 0))
                P.do("tensor", lambda E, b3=tt % 3: E.transpose(
                    psT[:, 0:128], krm[b3][:, 0:128], identb[:, :]))
                P.do("tensor", lambda E, b3=tt % 3: E.transpose(
                    psT[:, 128:256], krm[b3][:, 128:256], identb[:, :]),
                    inc=(None if own_tt else "ts"))
                if own_tt:
                    P.do("tensor", lambda E, b3=tt % 3: E.transpose(
                        psT[:, 256:384], qrm[b3][:, 0:128], identb[:, :]))
                    P.do("tensor", lambda E, b3=tt % 3: E.transpose(
                        psT[:, 384:512], qrm[b3][:, 128:256], identb[:, :]), inc="ts")
                tp_ts[tt] = P.cnt["ts"]
                P.W("vector", "ts", tp_ts[tt])
                P.do("vector", lambda E, tt=tt, KT=KT: E.tensor_copy(
                    KT[0][:, tt * 128:(tt + 1) * 128], psT[:, 0:128]))
                P.do("vector", lambda E, tt=tt, KT=KT: E.tensor_copy(
                    KT[1][:, tt * 128:(tt + 1) * 128], psT[:, 128:256]),
                    inc=(None if own_tt else "vs"))
                if own_tt:
                    P.do("vector", lambda E, tt=tt, QT=QT: E.tensor_copy(
                        QT[0][:, tt * 128:(tt + 1) * 128], psT[:, 256:384]))
                    P.do("vector", lambda E, tt=tt, QT=QT: E.tensor_copy(
                        QT[1][:, tt * 128:(tt + 1) * 128], psT[:, 384:512]), inc="vs")
                ptcopy[tt] = P.cnt["vs"]

            for t in range(NT):
                b3 = t % 3
                bi = t % 2
                own = t < ONT
                if t + 1 < NT:
                    if t >= 1:
                        P.W("sync", "ts", ts_mm[t - 1])
                    P.do("sync", lambda E, t=t: E.dma_start(
                        out=zt_s[(t + 1) % 2][:, :],
                        in_=zT_h.rearrange("(i p) s -> p i s", p=128)[:, :, (t + 1) * 128:(t + 2) * 128]),
                        inc="lds", n=16)
                    lds_zt_of[t + 1] = P.cnt["lds"]
                if t >= 2:
                    emit_tp(t - 2)
                nfill = 2 if own else 1
                for fill in range(nfill):
                    P.W("tensor", "lds", max(lds_zt_of[t], lds_wq))
                    P.W("tensor", "vs", last_evac[0])
                    for i in range(8):
                        if i == 4:
                            yield
                        st_, sp_ = (i == 0), (i == 7)
                        if fill == 0 and own:
                            P.do("tensor", lambda E, bi=bi, i=i, st_=st_, sp_=sp_: E.matmul(
                                psA[:, 0:512], zt_s[bi][:, i * 128:(i + 1) * 128],
                                wq_s[:, i * 768:i * 768 + 512], start=st_, stop=sp_))
                        elif fill == 0:
                            P.do("tensor", lambda E, bi=bi, i=i, st_=st_, sp_=sp_: E.matmul(
                                psA[:, 0:256], zt_s[bi][:, i * 128:(i + 1) * 128],
                                wq_s[:, i * 768:i * 768 + 256], start=st_, stop=sp_))
                            P.do("tensor", lambda E, bi=bi, i=i, st_=st_, sp_=sp_: E.matmul(
                                psA[:, 256:512], zt_s[bi][:, i * 128:(i + 1) * 128],
                                wq_s[:, i * 768 + 512:i * 768 + 768], start=st_, stop=sp_))
                        else:
                            P.do("tensor", lambda E, bi=bi, i=i, st_=st_, sp_=sp_: E.matmul(
                                psA[:, 0:256], zt_s[bi][:, i * 128:(i + 1) * 128],
                                wq_s[:, i * 768 + 512:i * 768 + 768], start=st_, stop=sp_))
                    if fill == 0:
                        P.do("tensor", lambda E, kc_=kc_: E.matmul(
                            psA[:, 0:256], onesb[:, :], brow_b[:, kc_:kc_ + 256],
                            start=False, stop=True))
                        if own:
                            P.do("tensor", lambda E, qc=qc: E.matmul(
                                psA[:, 256:512], onesb[:, :], brow_b[:, qc:qc + 256],
                                start=False, stop=True), inc="ts")
                        else:
                            P.do("tensor", lambda E, vc_=vc_: E.matmul(
                                psA[:, 256:512], onesb[:, :], brow_b[:, vc_:vc_ + 256],
                                start=False, stop=True), inc="ts")
                    else:
                        P.do("tensor", lambda E, vc_=vc_: E.matmul(
                            psA[:, 0:256], onesb[:, :], brow_b[:, vc_:vc_ + 256],
                            start=False, stop=True), inc="ts")
                    ts_mm[t] = P.cnt["ts"]

                    P.W("vector", "ts", ts_mm[t])
                    if fill == 0:
                        if t >= 3:
                            P.W("vector", "ts", tp_ts.get(t - 3, 0))
                            P.W("vector", "gsc", sq_done.get(t - 3, 0))
                        P.do("vector", lambda E, b3=b3: E.tensor_copy(
                            krm[b3][:, :], psA[:, 0:256]), inc="vs")
                        if own:
                            P.do("vector", lambda E, b3=b3: E.tensor_copy(
                                qrm[b3][:, :], psA[:, 256:512]), inc="vs")
                        else:
                            if direct_v:
                                P.do("vector", lambda E, t=t: E.tensor_copy(
                                    Vt.rearrange("p (tt h x) -> p tt h x", h=4, x=65)[:, t, :, 0:64],
                                    psA[:, 256:512].rearrange("p (h x) -> p h x", x=64)), inc="vs")
                            else:
                                P.W("vector", "gdma", vdma[0])
                                P.do("vector", lambda E, t=t: E.tensor_copy(
                                    vstg[t % 2].rearrange("p (h x) -> p h x", x=65)[:, :, 0:64],
                                    psA[:, 256:512].rearrange("p (h x) -> p h x", x=64)), inc="vs")
                    else:
                        if direct_v:
                            P.do("vector", lambda E, t=t: E.tensor_copy(
                                Vt.rearrange("p (tt h x) -> p tt h x", h=4, x=65)[:, t, :, 0:64],
                                psA[:, 0:256].rearrange("p (h x) -> p h x", x=64)), inc="vs")
                        else:
                            P.W("vector", "gdma", vdma[0])
                            P.do("vector", lambda E, t=t: E.tensor_copy(
                                vstg[t % 2].rearrange("p (h x) -> p h x", x=65)[:, :, 0:64],
                                psA[:, 0:256].rearrange("p (h x) -> p h x", x=64)), inc="vs")
                    evac_done[(t, fill)] = P.cnt["vs"]
                    last_evac[0] = P.cnt["vs"]

                    if (fill == 0 and not own) or fill == 1:
                        if not direct_v:
                            P.W("gpsimd", "vs", evac_done[(t, fill)])
                            P.W("gpsimd", "lds", vh_load_guard[0])
                            P.do("gpsimd", lambda E, t=t: E.dma_start(
                                out=v_h[:, t * 260:(t + 1) * 260],
                                in_=vstg[t % 2][:, :]), inc="gdma", n=16)
                            vdma[0] = P.cnt["gdma"]
                    if fill == 0:
                        rope(krm[b3], t)
                        if own:
                            P.selfw("vector", "vs")
                            rope(qrm[b3], t)
                        rope_done[t] = P.cnt["vs"]
                        P.W("gpsimd", "vs", rope_done[t])
                        if t >= 2:
                            P.W("gpsimd", "vs", nred.get(t - 2, 0))
                        P.do("gpsimd", lambda E, b3=b3, t=t: E.tensor_tensor(
                            sqg[t % 2].rearrange("p (g h d) -> p g h d", g=2, d=64)[:, 0],
                            krm[b3].rearrange("p (h d) -> p h d", d=64),
                            krm[b3].rearrange("p (h d) -> p h d", d=64), ALU.mult),
                            inc=(None if own else "gsc"))
                        if own:
                            P.do("gpsimd", lambda E, b3=b3, t=t: E.tensor_tensor(
                                sqg[t % 2].rearrange("p (g h d) -> p g h d", g=2, d=64)[:, 1],
                                qrm[b3].rearrange("p (h d) -> p h d", d=64),
                                qrm[b3].rearrange("p (h d) -> p h d", d=64), ALU.mult), inc="gsc")
                        sq_done[t] = P.cnt["gsc"]
                        P.W("vector", "gsc", sq_done[t])
                        P.do("vector", lambda E, t=t: E.reduce_sum(
                            normK2.rearrange("p (h t) -> p h t", t=NT)[:, :, t:t + 1],
                            _ins1(sqg[t % 2].rearrange("p (g h d) -> p g h d", g=2, d=64)[:, 0], 2),
                            axis=AX.X), inc=(None if own else "vs"))
                        if own:
                            P.do("vector", lambda E, t=t: E.reduce_sum(
                                normQ2.rearrange("p (h t) -> p h t", t=ONT)[:, :, t:t + 1],
                                _ins1(sqg[t % 2].rearrange("p (g h d) -> p g h d", g=2, d=64)[:, 1], 2),
                                axis=AX.X), inc="vs")
                        nred[t] = P.cnt["vs"]
                    yield
            emit_tp(NT - 2)
            emit_tp(NT - 1)
            a_wq_guard[0] = P.cnt["ts"]

            psM = psA
            P.selfw("vector", "vs")
            P.do("vector", lambda E: E.reduce_max(
                _ins1(mk2[:, :], 2),
                _ins1(normK2.rearrange("p (h t) -> p h t", t=NT), 2), axis=AX.X), inc="vs")
            P.do("vector", lambda E: E.reduce_max(
                _ins1(mq2[:, :], 2),
                _ins1(normQ2.rearrange("p (h t) -> p h t", t=ONT), 2), axis=AX.X), inc="vs")
            vs_m = P.cnt["vs"]
            P.W("tensor", "vs", vs_m)
            P.do("tensor", lambda E: E.transpose(psM[0:4, 0:128], mk2[:, :], ident32[:, :]))
            P.do("tensor", lambda E: E.transpose(psM[0:4, 128:256], mq2[:, :], ident32[:, :]),
                 inc="ts")
            ts_tp = P.cnt["ts"]
            P.W("vector", "ts", ts_tp)
            P.do("vector", lambda E: E.reduce_max(
                m1c[:, :], _ins1(psM[0:4, 0:128], 1), axis=AX.X))
            P.do("vector", lambda E: E.reduce_max(
                m2c[:, :], _ins1(psM[0:4, 128:256], 1), axis=AX.X), inc="vs")
            P.selfw("vector", "vs")
            P.do("vector", lambda E: E.tensor_tensor(
                mmv[:, :], m1c[:, :], m2c[:, :], ALU.mult), inc="vs")
            vs_mm = P.cnt["vs"]
            P.W("scalar", "vs", vs_mm)
            P.do("scalar", lambda E: E.activation(
                negb[:, :], mmv[:, :], AF.Ln), inc="asc")
            P.selfw("scalar", "asc")
            P.do("scalar", lambda E: E.activation(
                negb[:, :], negb[:, :], AF.Exp, scale=0.5), inc="asc")
            P.selfw("scalar", "asc")
            P.do("scalar", lambda E: E.mul(negb[:, :], negb[:, :], -0.125), inc="asc")
            asc_nb = P.cnt["asc"]
            P.W("tensor", "asc", asc_nb)
            P.do("tensor", lambda E: E.transpose(
                psM[0:1, 256:260], negb[:, :], ident32[0:4, 0:4]), inc="ts")
            P.W("vector", "ts")
            P.do("vector", lambda E: E.tensor_copy(negbrow[:, :], psM[0:1, 256:260]), inc="vs")
            P.W("tensor", "vs")
            P.do("tensor", lambda E: E.matmul(
                psM[:, 384:388], ones32[:, :], negbrow[:, :], start=True, stop=True),
                inc="ts")
            P.W("vector", "ts")
            P.do("vector", lambda E: E.tensor_copy(expBias[:, :], psM[:, 384:388]), inc="vs")
            P.selfw("vector", "vs")
            P.do("vector", lambda E, pairB=pairB: E.tensor_tensor(
                _ins1(pairB[:, :], 2),
                expBias.rearrange("p (a b) -> p a b", b=2)[:, :, 0:1],
                expBias.rearrange("p (a b) -> p a b", b=2)[:, :, 1:2],
                ALU.min), inc="vs")
            yield

        with contextlib.ExitStack() as esA0:
            psA0 = es_psum(nc, esA0, "psA0", [128, 512])
            psT0 = esA0.enter_context(nc.psum_tensor("psT0", [128, 512], BF16))
            P.W("gpsimd", "gsc")
            g0 = gen_A(0, psA0, psT0, True)
            for _ in g0:
                pass
            quiesce()
            barrier()

        for p in range(NPASS):
            with contextlib.ExitStack() as esB:
                scb = [es_psum(nc, esB, f"scb{p}_{i}", [128, 1024]) for i in range(2)]
                ctxE = es_psum(nc, esB, f"ctxE{p}", [65, 512])
                ctxO = es_psum(nc, esB, f"ctxO{p}", [65, 512])
                if p < NPASS - 1:
                    psA1 = es_psum(nc, esB, f"psA1_{p}", [128, 512])
                    psT1 = esB.enter_context(nc.psum_tensor(f"psT1_{p}", [128, 512], BF16))
                    agen = gen_A(p + 1, psA1, psT1, False)
                else:
                    agen = None
                acount = [0]

                def a_step():
                    if agen is not None:
                        try:
                            next(agen)
                            acount[0] += 1
                        except StopIteration:
                            pass

                vt_ld = {}
                if p >= 1:
                    for part in range(4):
                        P.do("sync", lambda E, part=part: E.dma_start(
                            out=Vt[:, part * 16 * 260:(part + 1) * 16 * 260],
                            in_=v_h[:, part * 16 * 260:(part + 1) * 16 * 260]),
                            inc="lds", n=16)
                        vt_ld[part] = P.cnt["lds"]
                    vh_load_guard[0] = P.cnt["lds"]
                else:
                    for part in range(4):
                        vt_ld[part] = 0

                KTp = KTs[p % 2]
                QTp = QTs[p % 2]
                pairBp = pairB2[p % 2]

                prev_pv = [0]
                prev_exp = [0]
                pending_tail = [None]
                prev_tail_tt = [P.cnt["vs"]]
                sc_done = {}
                exp_done = {}
                pv_done = {}
                ggc = [0]
                ccount = [0]
                for hp in range(2):
                    KTx, QTx = KTp[hp], QTp[hp]
                    hA, hB = 2 * hp, 2 * hp + 1
                    c = 2 * p + hp
                    for qb in range(4):
                        gg0 = ggc[0]

                        def emit_pv_chunk(k, qb=qb, gg0=gg0):
                            gk = gg0 + k
                            P.W("tensor", "asc", exp_done[gk])
                            P.W("tensor", "lds", vt_ld[min(k // 16, 3)])
                            if k == 0:
                                P.W("tensor", "vs", prev_tail_tt[0])
                            for ctx, cb, hloc in ((ctxE, 0, hA), (ctxO, 512, hB)):
                                P.do("tensor", lambda E, k=k, gk=gk, ctx=ctx, cb=cb, hloc=hloc: E.matmul(
                                    ctx[:, :],
                                    Vt[:, k * 260 + hloc * 65: k * 260 + hloc * 65 + 65],
                                    PAB[gk % 5][:, cb:cb + 512],
                                    start=(k == 0), stop=(k == 63)),
                                    inc=("ts" if cb == 512 else None))
                            pv_done[gk] = P.cnt["ts"]

                        for g in range(NT):
                            gg = gg0 + g
                            if gg >= 2:
                                P.W("tensor", "asc", exp_done[gg - 2])
                            for KT_lo, cb in ((0, 0), (64, 512)):
                                P.do("tensor", lambda E, g=g, gg=gg, KT_lo=KT_lo, cb=cb, qb=qb,
                                     KTx=KTx, QTx=QTx: E.matmul(
                                    scb[gg % 2][:, cb:cb + 512],
                                    KTx[KT_lo:KT_lo + 64, g * 128:(g + 1) * 128],
                                    QTx[KT_lo:KT_lo + 64, qb * 512:(qb + 1) * 512],
                                    start=True, stop=True,
                                    tile_position=(KT_lo, 0)),
                                    inc=("ts" if KT_lo == 64 else None))
                            sc_done[gg] = P.cnt["ts"]
                            if g == 2 and pending_tail[0] is not None:
                                pending_tail[0]()
                                pending_tail[0] = None
                            if g >= 2:
                                emit_pv_chunk(g - 2)
                            P.W("scalar", "ts", sc_done[gg])
                            if gg >= 5:
                                P.W("scalar", "ts", pv_done[gg - 5])
                            P.do("scalar", lambda E, gg=gg, hp=hp, scb=scb, pairBp=pairBp: E.activation(
                                PAB[gg % 5][:, :], scb[gg % 2][:, :], AF.Exp,
                                bias=pairBp[:, hp:hp + 1], scale=0.125), inc="asc")
                            exp_done[gg] = P.cnt["asc"]
                            ccount[0] += 1
                            if ccount[0] % 3 == 0:
                                a_step()
                        emit_pv_chunk(NT - 2)
                        emit_pv_chunk(NT - 1)
                        ggc[0] += NT

                        ts_pv_last = P.cnt["ts"]
                        P.W("scalar", "ts", ts_pv_last)
                        P.do("scalar", lambda E, ctxE=ctxE: E.activation(
                            lnscr[:, 0:512], ctxE[64:65, :], AF.Ln))
                        P.do("scalar", lambda E, ctxO=ctxO: E.activation(
                            lnscr[:, 512:1024], ctxO[64:65, :], AF.Ln), inc="asc")
                        P.selfw("scalar", "asc")
                        P.do("scalar", lambda E: E.activation(
                            recEf[:, :], lnscr[:, 0:512], AF.Exp, scale=-1.0))
                        P.do("scalar", lambda E: E.activation(
                            recOf[:, :], lnscr[:, 512:1024], AF.Exp, scale=-1.0), inc="asc")
                        asc_rec = P.cnt["asc"]
                        P.W("vector", "asc", asc_rec)
                        P.W("vector", "ts", ts_pv_last)
                        P.do("vector", lambda E: E.tensor_copy(recEb[:, :], recEf[:, :]))
                        P.do("vector", lambda E: E.tensor_copy(recOb[:, :], recOf[:, :]))
                        P.do("vector", lambda E, ctxE=ctxE: E.tensor_copy(stgE[:, :], ctxE[0:64, :]))
                        if qb > 0 or hp > 0 or p > 0:
                            P.W("vector", "gdma")
                        P.do("vector", lambda E, ctxO=ctxO: E.tensor_copy(stgO[:, :], ctxO[0:64, :]), inc="vs")
                        vs_raw = P.cnt["vs"]

                        def make_tail(vs_raw=vs_raw, c=c, qb=qb, ctxE=ctxE, ctxO=ctxO):
                            def tail():
                                P.W("tensor", "vs", vs_raw)
                                P.do("tensor", lambda E, ctxE=ctxE: E.matmul(
                                    ctxE[0:64, :], onesb[0:1, 0:64], recEb[:, :],
                                    start=True, stop=True))
                                P.do("tensor", lambda E, ctxO=ctxO: E.matmul(
                                    ctxO[0:64, :], onesb[0:1, 0:64], recOb[:, :],
                                    start=True, stop=True), inc="ts")
                                ts_bc = P.cnt["ts"]
                                P.W("vector", "ts", ts_bc)
                                P.do("vector", lambda E, ctxE=ctxE: E.tensor_tensor(
                                    ctx_store[0:64, c * OWN + qb * 512: c * OWN + (qb + 1) * 512],
                                    stgE[:, :], ctxE[0:64, :], ALU.mult))
                                P.do("vector", lambda E, ctxO=ctxO: E.tensor_tensor(
                                    stgOn[:, :], stgO[:, :], ctxO[0:64, :], ALU.mult), inc="vs")
                                prev_tail_tt[0] = P.cnt["vs"]
                                P.W("gpsimd", "vs", prev_tail_tt[0])
                                P.do("gpsimd", lambda E: E.dma_start(
                                    out=ctx_store[64:128, c * OWN + qb * 512: c * OWN + (qb + 1) * 512],
                                    in_=stgOn[:, :]), inc="gdma", n=16)
                            return tail

                        pending_tail[0] = make_tail()

                if pending_tail[0] is not None:
                    pending_tail[0]()
                    pending_tail[0] = None
                if agen is not None:
                    for _ in agen:
                        pass
                quiesce()
                barrier()

    with contextlib.ExitStack() as es5:
        sb5 = lambda name, shp, dt: es5.enter_context(nc.sbuf_tensor(name, shp, dt))
        wout_sb = sb5("wout_sb", [128, 8 * D], BF16)
        wfT_sb = sb5("wfT_sb", [128, 8 * D], BF16)
        boutrow = sb5("boutrow", [1, D], BF16)
        xt2 = [sb5(f"xt2_{i}", [128, D], F32) for i in range(2)]
        r1 = [sb5(f"r1_{i}", [128, D], F32) for i in range(2)]
        r2 = [sb5(f"r2_{i}", [128, D], F32) for i in range(2)]
        y1 = [sb5(f"y1_{i}", [128, D], BF16) for i in range(2)]
        y1T = [sb5(f"y1T_{i}", [128, D], BF16) for i in range(2)]
        y2 = [sb5(f"y2_{i}", [128, D], F32) for i in range(2)]
        bns5 = [sb5(f"bns5_{i}", [128, 12], F32) for i in range(2)]
        st5 = [sb5(f"st5_{i}", [128, 2], F32) for i in range(2)]
        ln5 = [sb5(f"ln5_{i}", [128, 1], F32) for i in range(2)]
        rstd5 = [sb5(f"rstd5_{i}", [128, 1], F32) for i in range(2)]
        negm5 = [sb5(f"negm5_{i}", [128, 1], F32) for i in range(2)]
        ps_po = [es_psum(nc, es5, "ps_po0", [128, 1024])] * 2
        ps_ffn = [es_psum(nc, es5, f"ps_ffn{i}", [128, 1024]) for i in range(2)]
        ps_t5 = es5.enter_context(nc.psum_tensor("ps_t5", [128, 1024], BF16))

        P.do("sync", lambda E: E.dma_start(
            out=wout_sb.rearrange("p (c d) -> p c d", d=D),
            in_=wo_h.rearrange("(c p) d -> p c d", p=128)), inc="lds", n=16)
        P.do("sync", lambda E: E.dma_start(
            out=wfT_sb.rearrange("p (c d) -> p c d", d=D),
            in_=wf_h.rearrange("(c p) d -> p c d", p=128)), inc="lds", n=16)
        P.do("gpsimd", lambda E: E.dma_start(out=boutrow[:, :], in_=BoutRow[:, :]), inc="gdma", n=16)
        P.W("tensor", "gdma")
        lds_w5 = P.cnt["lds"]

        sub_state = {}
        gi = [0]
        for sbk in range(4):
            for sub in range(4):
                g = gi[0]; gi[0] += 1
                pb = g % 2
                t5 = sbk * 4 + sub
                P.W("sync", "vs")
                P.do("sync", lambda E, t5=t5, pb=pb: E.dma_start(
                    out=xt2[pb][:, :], in_=X[t5 * 128:(t5 + 1) * 128, :]), inc="lds", n=16)
                lds_x = P.cnt["lds"]

                P.W("tensor", "lds", lds_w5)
                if g >= 1:
                    P.W("tensor", "vs", sub_state[g - 1]["r1_add"])
                for dh in range(2):
                    for cch in range(8):
                        P.do("tensor", lambda E, t5=t5, cch=cch, dh=dh, pb=pb: E.matmul(
                            ps_po[pb][:, dh * 512:(dh + 1) * 512],
                            ctx_store[:, cch * OWN + t5 * 128: cch * OWN + (t5 + 1) * 128],
                            wout_sb[:, cch * D + dh * 512: cch * D + (dh + 1) * 512],
                            start=(cch == 0), stop=False))
                    P.do("tensor", lambda E, dh=dh, pb=pb: E.matmul(
                        ps_po[pb][:, dh * 512:(dh + 1) * 512],
                        onesb[:, :], boutrow[:, dh * 512:(dh + 1) * 512],
                        start=False, stop=True),
                        inc=("ts" if dh == 1 else None))
                ts_po = P.cnt["ts"]

                P.W("vector", "ts", ts_po)
                P.W("vector", "lds", lds_x)
                if g >= 2:
                    P.W("vector", "asc", sub_state[g - 2]["y1_apply"])
                P.do("vector", lambda E, pb=pb: E.scalar_tensor_tensor(
                    r1[pb][:, :], xt2[pb][:, :], 2.0, ps_po[pb][:, :],
                    ALU.mult, ALU.add), inc="vs")
                P.selfw("vector", "vs")

                def ln_block(srcb, pb):
                    P.do("vector", lambda E, srcb=srcb, pb=pb: E.bn_stats(
                        bns5[pb][:, 0:6], srcb[:, 0:512]))
                    P.do("vector", lambda E, srcb=srcb, pb=pb: E.bn_stats(
                        bns5[pb][:, 6:12], srcb[:, 512:1024]), inc="vs")
                    P.selfw("vector", "vs")
                    if g >= 2:
                        P.W("vector", "asc", sub_state[g - 2]["y2_apply"])
                    P.do("vector", lambda E, pb=pb: E.bn_aggr(st5[pb][:, :], bns5[pb][:, :]),
                         inc="vs")
                    vs_a = P.cnt["vs"]
                    P.W("scalar", "vs", vs_a)
                    P.do("scalar", lambda E, pb=pb: E.activation(
                        ln5[pb][:, :], st5[pb][:, 1:2], AF.Ln, bias=epst[:, 0:1]), inc="asc")
                    P.selfw("scalar", "asc")
                    P.do("scalar", lambda E, pb=pb: E.activation(
                        rstd5[pb][:, :], ln5[pb][:, :], AF.Exp, scale=-0.5), inc="asc")
                    asc_r = P.cnt["asc"]
                    P.W("vector", "asc", asc_r)
                    P.do("vector", lambda E, pb=pb: E.tensor_scalar(
                        negm5[pb][:, :], st5[pb][:, 0:1], rstd5[pb][:, 0:1], -1.0,
                        ALU.mult, ALU.mult), inc="vs")
                    return P.cnt["vs"]

                vs_n1 = ln_block(r1[pb], pb)
                P.W("scalar", "vs", vs_n1)
                if g >= 2:
                    P.W("scalar", "ts", sub_state[g - 2]["ffn_done"])
                P.do("scalar", lambda E, pb=pb: E.activation(
                    y1[pb][:, :], r1[pb][:, :], AF.Identity, bias=negm5[pb][:, 0:1],
                    scale=rstd5[pb][:, 0:1]), inc="asc")
                y1_apply = P.cnt["asc"]

                P.W("tensor", "asc", y1_apply)
                if g >= 1:
                    P.W("tensor", "vs", sub_state[g - 1]["y1T_evac"])
                for j in range(8):
                    P.do("tensor", lambda E, j=j, pb=pb: E.transpose(
                        ps_t5[:, j * 128:(j + 1) * 128],
                        y1[pb][:, j * 128:(j + 1) * 128], identb[:, :]),
                        inc=("ts" if j == 7 else None))
                ts_t5 = P.cnt["ts"]
                P.W("vector", "ts", ts_t5)
                if g >= 2:
                    P.W("vector", "ts", sub_state[g - 2]["ffn_done"])
                P.do("vector", lambda E, pb=pb: E.tensor_copy(
                    y1T[pb][:, :], ps_t5[:, :]), inc="vs")
                y1T_evac = P.cnt["vs"]

                P.W("tensor", "vs", y1T_evac)
                if g >= 2:
                    P.W("tensor", "vs", sub_state[g - 2]["r2_evac"])
                for ob in range(2):
                    for j in range(8):
                        P.do("tensor", lambda E, ob=ob, j=j, pb=pb: E.matmul(
                            ps_ffn[pb][:, ob * 512:(ob + 1) * 512],
                            y1T[pb][:, j * 128:(j + 1) * 128],
                            wfT_sb[:, j * D + ob * 512: j * D + (ob + 1) * 512],
                            start=(j == 0), stop=(j == 7)),
                            inc=("ts" if (ob == 1 and j == 7) else None))
                ffn_done = P.cnt["ts"]
                P.W("vector", "ts", ffn_done)
                P.do("vector", lambda E, pb=pb: E.tensor_tensor(
                    r2[pb][:, 0:512], ps_ffn[pb][:, 0:512], y1[pb][:, 0:512], ALU.add))
                P.do("vector", lambda E, pb=pb: E.tensor_tensor(
                    r2[pb][:, 512:1024], ps_ffn[pb][:, 512:1024], y1[pb][:, 512:1024],
                    ALU.add), inc="vs")
                r2_evac = P.cnt["vs"]
                P.selfw("vector", "vs")
                vs_n2 = ln_block(r2[pb], pb)
                P.W("scalar", "vs", vs_n2)
                if g >= 2:
                    P.W("scalar", "sto", 16 * (g - 1))
                P.do("scalar", lambda E, pb=pb: E.activation(
                    y2[pb][:, :], r2[pb][:, :], AF.Identity, bias=negm5[pb][:, 0:1],
                    scale=rstd5[pb][:, 0:1]), inc="asc")
                y2_apply = P.cnt["asc"]
                P.W("sync", "asc", y2_apply)
                P.do("sync", lambda E, t5=t5, pb=pb: E.dma_start(
                    out=OUT[t5 * 128:(t5 + 1) * 128, :], in_=y2[pb][:, :]), inc="sto", n=16)
                sub_state[g] = dict(y1_apply=y1_apply, y1T_evac=y1T_evac,
                                    ffn_done=ffn_done, r2_evac=r2_evac,
                                    y2_apply=y2_apply, r1_add=P.cnt["vs"])

        P.W("sync", "sto")

    with nc.Block() as block:
        @block.sync
        def _(sync):
            for fn in P.ops["sync"]:
                fn(sync)

        @block.gpsimd
        def _(gpsimd):
            for fn in P.ops["gpsimd"]:
                fn(gpsimd)

        @block.vector
        def _(vector):
            for fn in P.ops["vector"]:
                fn(vector)

        @block.tensor
        def _(tensor):
            for fn in P.ops["tensor"]:
                fn(tensor)

        @block.scalar
        def _(scalar):
            for fn in P.ops["scalar"]:
                fn(scalar)

    es.close()
    return nc


def es_psum(nc, es, name, shp):
    return es.enter_context(nc.psum_tensor(name, shp, F32))


def _rope_tables():
    RD = 32
    inv = (1.0 / 10000.0 ** (np.arange(0, RD, 2, dtype=np.float32) / np.float32(RD))).astype(np.float32)
    t = np.arange(S, dtype=np.float32)
    ang = (t[:, None] * inv[None, :]).astype(np.float32)
    fr = np.repeat(ang, 2, axis=1).astype(np.float64)
    cos = np.cos(fr).astype(np.float32)
    sin = np.sin(fr)
    sgn = np.ones((1, RD))
    sgn[0, 0::2] = -1.0
    sins = (sin * sgn).astype(np.float32)
    return cos, sins


_NC_CACHE = {}


def kernel(L0_attn_X, L0_attn_W_in, L0_attn_b_in, L0_attn_W_out, L0_attn_b_out,
           L0_attn_gamma, L0_attn_beta, L0_ffn_weight):
    Xf = np.ascontiguousarray(np.asarray(L0_attn_X, dtype=np.float32))
    Win = np.ascontiguousarray(np.asarray(L0_attn_W_in, dtype=np.float32))
    bin_ = np.ascontiguousarray(np.asarray(L0_attn_b_in, dtype=np.float32).reshape(1, 3 * D))
    Wout = np.ascontiguousarray(np.asarray(L0_attn_W_out, dtype=np.float32))
    bout = np.ascontiguousarray(np.asarray(L0_attn_b_out, dtype=np.float32).reshape(8, 128).T)
    gam = np.ascontiguousarray(np.asarray(L0_attn_gamma, dtype=np.float32).reshape(8, 128).T)
    bet = np.ascontiguousarray(np.asarray(L0_attn_beta, dtype=np.float32).reshape(8, 128).T)
    Wffn = np.ascontiguousarray(np.asarray(L0_ffn_weight, dtype=np.float32))
    cos, sins = _rope_tables()

    if "nc" not in _NC_CACHE:
        _NC_CACHE["nc"] = _build()
    nc = _NC_CACHE["nc"]

    in_maps = []
    for core in range(8):
        b, q = core // 4, core % 4
        Xb = Xf[b]
        perm = np.concatenate([np.arange(q * OWN, (q + 1) * OWN),
                               np.arange(0, q * OWN),
                               np.arange((q + 1) * OWN, S)])
        in_maps.append({
            "X": np.ascontiguousarray(Xb[perm]),
            "BoutRow": np.ascontiguousarray(np.asarray(L0_attn_b_out, dtype=np.float32).reshape(1, D)),
            "Win": Win, "BinRow": bin_, "Wout": Wout, "BoutCol": bout,
            "GamCol": gam, "BetCol": bet, "Wffn": Wffn,
            "RopeC": np.ascontiguousarray(cos[perm]),
            "RopeS": np.ascontiguousarray(sins[perm]),
        })
    res = run_bass_kernel_spmd(nc, in_maps, core_ids=list(range(8)))
    out = np.empty((2, S, D), dtype=np.float32)
    for core in range(8):
        b, q = core // 4, core % 4
        out[b, q * OWN:(q + 1) * OWN] = res.results[core]["OUT"]
    return out


# version 35
# speedup vs baseline: 1.1787x; 1.1787x over previous
import contextlib
import numpy as np
import ml_dtypes
import concourse.bass as bass
import concourse.mybir as mybir
from concourse.bass_utils import run_bass_kernel_spmd

F32 = mybir.dt.float32
BF16 = mybir.dt.bfloat16
AF = mybir.ActivationFunctionType
ALU = mybir.AluOpType
AX = mybir.AxisListType

S = 8192
D = 1024
NT = 64
OWN = 2048
ONT = 16
NPASS = 4
HPP = 4
GK = 3
NG = 22
EPS = 1e-5

ENGS = ("sync", "gpsimd", "vector", "tensor", "scalar")


class _Prog:

    def __init__(self, nc, es):
        self.nc = nc
        self.es = es
        self.ops = {e: [] for e in ENGS}
        self.sem = {}
        self.cnt = {}

    def new_sem(self, name):
        self.sem[name] = self.es.enter_context(self.nc.semaphore(name))
        self.cnt[name] = 0

    def W(self, eng, name, val=None):
        v = self.cnt[name] if val is None else val
        s = self.sem[name]
        self.ops[eng].append(lambda E, s=s, v=v: E.wait_ge(s, v))

    def do(self, eng, fn, inc=None, n=1):
        if inc is not None:
            self.cnt[inc] += n
            s = self.sem[inc]
            self.ops[eng].append(lambda E, fn=fn, s=s, n=n: fn(E).then_inc(s, n))
        else:
            self.ops[eng].append(fn)

    def selfw(self, eng, name):
        self.W(eng, name)

    def bump(self, eng, name, n=1):
        self.cnt[name] += n
        s = self.sem[name]
        self.ops[eng].append(lambda E, s=s, n=n: E.sem_inc(s, n))


def _ins1(ap, axis):
    pat = [list(d) for d in ap.ap]
    pat = pat[:axis] + [[0, 1]] + pat[axis:]
    return bass.AP(ap.tensor, ap.offset, pat)


def _build():
    nc = bass.Bass("TRN2", target_bir_lowering=False, debug=False)

    X = nc.dram_tensor("X", [S, D], F32, kind="ExternalInput")
    Win = nc.dram_tensor("Win", [D, 3 * D], F32, kind="ExternalInput")
    BinRow = nc.dram_tensor("BinRow", [1, 3 * D], F32, kind="ExternalInput")
    Wout = nc.dram_tensor("Wout", [D, D], F32, kind="ExternalInput")
    BoutCol = nc.dram_tensor("BoutCol", [128, 8], F32, kind="ExternalInput")
    BoutRow = nc.dram_tensor("BoutRow", [1, D], F32, kind="ExternalInput")
    GamCol = nc.dram_tensor("GamCol", [128, 8], F32, kind="ExternalInput")
    BetCol = nc.dram_tensor("BetCol", [128, 8], F32, kind="ExternalInput")
    Wffn = nc.dram_tensor("Wffn", [D, D], F32, kind="ExternalInput")
    RopeC = nc.dram_tensor("RopeC", [S, 32], F32, kind="ExternalInput")
    RopeS = nc.dram_tensor("RopeS", [S, 32], F32, kind="ExternalInput")
    OUT = nc.dram_tensor("OUT", [OWN, D], F32, kind="ExternalOutput")

    zT_h = nc.dram_tensor("zT_h", [D, S], BF16, kind="Internal")
    wq_h = nc.dram_tensor("wq_h", [D, 3 * D], BF16, kind="Internal")
    wo_h = nc.dram_tensor("wo_h", [D, D], BF16, kind="Internal")
    wf_h = nc.dram_tensor("wf_h", [D, D], BF16, kind="Internal")

    es = contextlib.ExitStack()
    P = _Prog(nc, es)
    for s in ("lds", "lda", "ldt", "gdma", "stz", "sto", "vs", "ts", "asc", "gsc", "bar",
              "lz0", "lz1", "lz2", "sz0", "sz1", "sz2"):
        P.new_sem(s)

    sb = lambda name, shp, dt: es.enter_context(nc.sbuf_tensor(name, shp, dt))

    ctx_store = sb("ctx_store", [128, 8 * OWN], BF16)
    ropeCb = sb("ropeCb", [128, NT * 32], BF16)
    ropeSb = sb("ropeSb", [128, NT * 32], BF16)
    brow_b = sb("brow_b", [1, 3 * D], BF16)
    ident32 = sb("ident32", [128, 128], F32)
    identb = sb("identb", [128, 128], BF16)
    ones32 = sb("ones32", [1, 128], F32)
    onesb = sb("onesb", [1, 128], BF16)
    epst = sb("epst", [128, 1], F32)
    gam = sb("gam", [128, 8], F32)
    bet = sb("bet", [128, 8], F32)

    barN = [0]

    def barrier():
        for e in ENGS:
            P.bump(e, "bar")
        barN[0] += 1
        for e in ENGS:
            P.W(e, "bar", 5 * barN[0])

    def quiesce():
        P.W("sync", "lds")
        P.W("sync", "ldt")
        P.W("sync", "stz")
        P.W("sync", "sto")
        P.W("scalar", "lda")
        P.W("scalar", "asc")
        P.W("gpsimd", "gdma")
        P.W("gpsimd", "gsc")
        P.W("vector", "vs")
        P.W("tensor", "ts")

    with contextlib.ExitStack() as es0:
        sb0 = lambda name, shp, dt: es0.enter_context(nc.sbuf_tensor(name, shp, dt))
        wtmp32 = [sb0(f"wtmp32_{i}", [128, 3 * D], F32) for i in range(2)]
        wtmpb = [sb0(f"wtmpb{i}", [128, 3 * D], BF16) for i in range(2)]
        wqb = [sb0(f"wqb{i}", [128, 3 * D], BF16) for i in range(2)]
        woutb = [sb0(f"woutb{i}", [128, D], BF16) for i in range(2)]
        wot32 = [sb0(f"wot32_{i}", [128, D], F32) for i in range(2)]
        wft32 = [sb0(f"wft32_{i}", [128, D], F32) for i in range(2)]
        wsignb = [sb0(f"wsignb{i}", [128, D], BF16) for i in range(2)]
        wtr = sb0("wtr", [128, D], BF16)
        binrow = sb0("binrow", [1, 3 * D], F32)
        betb = sb0("betb", [128, 8], BF16)
        brow_ps = es0.enter_context(nc.psum_tensor("brow_ps", [1, 3 * D], F32))

        P.do("sync", lambda E: E.dma_start(out=gam[:, :], in_=GamCol[:, :]), inc="lds", n=16)
        P.do("sync", lambda E: E.dma_start(out=bet[:, :], in_=BetCol[:, :]), inc="lds", n=16)
        P.do("sync", lambda E: E.dma_start(out=binrow[:, :], in_=BinRow[:, :]), inc="lds", n=16)
        P.do("gpsimd", lambda E: E.dma_start(
            out=ropeCb.rearrange("p (t c) -> p t c", c=32),
            in_=RopeC.rearrange("(t p) c -> p t c", p=128)), inc="gdma", n=16)
        P.do("gpsimd", lambda E: E.dma_start(
            out=ropeSb.rearrange("p (t c) -> p t c", c=32),
            in_=RopeS.rearrange("(t p) c -> p t c", p=128)), inc="gdma", n=16)
        P.do("gpsimd", lambda E: E.memset(ident32[:, :], 0.0))
        P.do("gpsimd", lambda E: E.affine_select(
            out=ident32[:, :], in_=ident32[:, :], compare_op=ALU.not_equal,
            fill=1.0, base=0, pattern=[[-1, 128]], channel_multiplier=1))
        P.do("gpsimd", lambda E: E.memset(ones32[:, :], 1.0), inc="gsc")
        P.W("vector", "gsc")
        P.do("vector", lambda E: E.tensor_copy(identb[:, :], ident32[:, :]), inc="vs")
        P.do("gpsimd", lambda E: E.memset(onesb[:, :], 1.0))
        P.do("gpsimd", lambda E: E.memset(epst[:, :], EPS), inc="gsc")

        P.W("vector", "lds", 32)
        P.do("vector", lambda E: E.tensor_copy(betb[:, :], bet[:, :]), inc="vs")
        vs_betb = P.cnt["vs"]

        vs_scale = {}
        ts_brow = {}
        for i in range(8):
            if i >= 2:
                P.W("sync", "vs", vs_scale[i - 2])
            P.do("sync", lambda E, i=i: E.dma_start(
                out=wtmp32[i % 2][:, :], in_=Win[i * 128:(i + 1) * 128, :]), inc="lds", n=16)
            P.W("vector", "lds")
            if i >= 2:
                P.W("vector", "ts", ts_brow[i - 2])
            P.do("vector", lambda E, i=i: E.tensor_copy(wtmpb[i % 2][:, :], wtmp32[i % 2][:, :]), inc="vs")
            P.W("tensor", "vs")
            P.W("tensor", "vs", vs_betb)
            for j in range(6):
                P.do("tensor", lambda E, i=i, j=j: E.matmul(
                    brow_ps[0:1, j * 512:(j + 1) * 512], betb[:, i:i + 1],
                    wtmpb[i % 2][:, j * 512:(j + 1) * 512],
                    start=(i == 0), stop=(i == 7)), inc=("ts" if j == 5 else None))
            ts_brow[i] = P.cnt["ts"]
            if i >= 2:
                P.W("vector", "stz")
            P.do("vector", lambda E, i=i: E.tensor_scalar(
                wqb[i % 2][:, :], wtmp32[i % 2][:, :], gam[:, i:i + 1], None, ALU.mult), inc="vs")
            vs_scale[i] = P.cnt["vs"]
            P.W("sync", "vs", vs_scale[i])
            P.do("sync", lambda E, i=i: E.dma_start(
                out=wq_h[i * 128:(i + 1) * 128, :], in_=wqb[i % 2][:, :]), inc="stz", n=16)
        P.W("vector", "ts")
        P.W("vector", "lds", 48)
        P.do("vector", lambda E: E.tensor_tensor(
            brow_b[:, :], brow_ps[0:1, :], binrow[:, :], ALU.add), inc="vs")

        vs_wout = {}
        lda_wout = {}
        for i in range(8):
            if i >= 2:
                P.W("scalar", "vs", vs_wout[i - 2])
            P.do("scalar", lambda E, i=i: E.dma_start(
                out=wot32[i % 2][:, :], in_=Wout[i * 128:(i + 1) * 128, :]), inc="lda", n=16)
            lda_wout[i] = P.cnt["lda"]
            P.W("vector", "lda", lda_wout[i])
            if i >= 2:
                P.W("vector", "stz")
            P.do("vector", lambda E, i=i: E.tensor_copy(woutb[i % 2][:, :], wot32[i % 2][:, :]), inc="vs")
            vs_wout[i] = P.cnt["vs"]
            P.W("sync", "vs", vs_wout[i])
            P.do("sync", lambda E, i=i: E.dma_start(
                out=wo_h[i * 128:(i + 1) * 128, :], in_=woutb[i % 2][:, :]), inc="stz", n=16)

        ldt_ffn = {}
        for oc in range(8):
            P.do("scalar", lambda E, oc=oc: E.dma_start(
                out=wft32[oc % 2][:, :], in_=Wffn[oc * 128:(oc + 1) * 128, :]), inc="lda", n=16)
            P.W("scalar", "lda")
            if oc >= 2:
                P.W("scalar", "ldt", ldt_ffn[oc - 2])
            P.do("scalar", lambda E, oc=oc: E.sign(wsignb[oc % 2][:, :], wft32[oc % 2][:, :]), inc="asc")
            P.W("sync", "asc")
            P.W("sync", "stz")
            for j in range(8):
                P.do("sync", lambda E, j=j, oc=oc: E.dma_start(
                    out=wtr[:, j * 128:(j + 1) * 128],
                    in_=wsignb[oc % 2][:, j * 128:(j + 1) * 128], transpose=True),
                    inc="ldt", n=16)
            ldt_ffn[oc] = P.cnt["ldt"]
            P.W("sync", "ldt")
            P.do("sync", lambda E, oc=oc: E.dma_start(
                out=wf_h.rearrange("(j p) o -> p j o", p=128)[:, :, oc * 128:(oc + 1) * 128],
                in_=wtr.rearrange("p (j q) -> p j q", q=128)), inc="stz", n=16)

        sb1 = sb0
        es1 = es0
        NZ = 3
        xt = [sb1(f"xt{i}", [128, D], F32) for i in range(NZ)]
        zb = [sb1(f"zb{i}", [128, D], BF16) for i in range(NZ)]
        ztmp = [sb1(f"ztmp{i}", [128, D], BF16) for i in range(NZ)]
        bns = [sb1(f"bns{i}", [128, 12], F32) for i in range(NZ)]
        stats = [sb1(f"stats{i}", [128, 2], F32) for i in range(NZ)]
        lnt = [sb1(f"lnt{i}", [128, 1], F32) for i in range(NZ)]
        rstd = [sb1(f"rstd{i}", [128, 1], F32) for i in range(NZ)]
        negmr = [sb1(f"negmr{i}", [128, 1], F32) for i in range(NZ)]
        ztp = [es1.enter_context(nc.psum_tensor(f"ztp{i}", [128, 1024], BF16)) for i in range(2)]

        lds0 = P.cnt["lds"]
        vs_done = {}
        asc_done = {}
        ts_done = {}
        vse_done = {}
        stz_done = {}
        lzs = [P.sem["lz0"], P.sem["lz1"], P.sem["lz2"]]
        szs = [P.sem["sz0"], P.sem["sz1"], P.sem["sz2"]]
        lzc = {0: 0, 1: 0, 2: 0}
        szc = {0: 0, 1: 0, 2: 0}
        vs_done = {}
        asc_done = {}
        ts_done = {}
        vse_done = {}
        for t0_ in range(NZ):
            P.do("gpsimd", lambda E, t=t0_: E.dma_start(
                out=xt[t % NZ][:, :], in_=X[t * 128:(t + 1) * 128, :]).then_inc(
                lzs[t % NZ], 16) or None)
            lzc[t0_ % NZ] += 16
        for t in range(NT):
            bz = t % NZ
            self_lz = lzc[bz]
            P.ops["vector"].append(lambda E, s=lzs[bz], v=self_lz: E.wait_ge(s, v))
            P.do("vector", lambda E, bz=bz: E.bn_stats(bns[bz][:, 0:6], xt[bz][:, 0:512]))
            P.do("vector", lambda E, bz=bz: E.bn_stats(bns[bz][:, 6:12], xt[bz][:, 512:1024]), inc="vs")
            P.selfw("vector", "vs")
            if t >= NZ:
                P.W("vector", "asc", asc_done[t - NZ])
            P.do("vector", lambda E, bz=bz: E.bn_aggr(stats[bz][:, :], bns[bz][:, :]), inc="vs")
            vs_aggr = P.cnt["vs"]

            P.W("scalar", "vs", vs_aggr)
            P.do("scalar", lambda E, bz=bz: E.activation(
                lnt[bz][:, :], stats[bz][:, 1:2], AF.Ln, bias=epst[:, 0:1]), inc="asc")
            P.selfw("scalar", "asc")
            P.do("scalar", lambda E, bz=bz: E.activation(
                rstd[bz][:, :], lnt[bz][:, :], AF.Exp, scale=-0.5), inc="asc")
            asc_rstd = P.cnt["asc"]

            P.W("vector", "asc", asc_rstd)
            P.do("vector", lambda E, bz=bz: E.tensor_scalar(
                negmr[bz][:, :], stats[bz][:, 0:1], rstd[bz][:, 0:1], -1.0, ALU.mult, ALU.mult),
                inc="vs")
            vs_done[t] = P.cnt["vs"]

            P.W("scalar", "vs", vs_done[t])
            if t >= 2:
                P.W("scalar", "ts", ts_done[t - 2])
            P.do("scalar", lambda E, bz=bz: E.activation(
                zb[bz % 2 if False else bz][:, :], xt[bz][:, :], AF.Identity,
                bias=negmr[bz][:, 0:1], scale=rstd[bz][:, 0:1]), inc="asc")
            asc_done[t] = P.cnt["asc"]

            P.W("tensor", "asc", asc_done[t])
            if t >= 2:
                P.W("tensor", "vs", vse_done[t - 2])
            for j in range(8):
                P.do("tensor", lambda E, bz=bz, j=j: E.transpose(
                    ztp[t % 2 if False else (bz % 2)][:, j * 128:(j + 1) * 128],
                    zb[bz][:, j * 128:(j + 1) * 128], identb[:, :]),
                    inc=("ts" if j == 7 else None))
            ts_done[t] = P.cnt["ts"]

            P.W("vector", "ts", ts_done[t])
            if t >= NZ:
                v = szc[bz]
                P.ops["vector"].append(lambda E, s=szs[bz], v=v: E.wait_ge(s, v))
            P.do("vector", lambda E, bz=bz: E.tensor_copy(
                ztmp[bz][:, :], ztp[bz % 2][:, :]), inc="vs")
            vse_done[t] = P.cnt["vs"]

            P.W("gpsimd", "vs", vse_done[t])
            P.do("gpsimd", lambda E, t=t, bz=bz: E.dma_start(
                out=zT_h.rearrange("(j p) s -> p j s", p=128)[:, :, t * 128:(t + 1) * 128],
                in_=ztmp[bz].rearrange("p (j q) -> p j q", q=128)).then_inc(
                szs[bz], 16) or None)
            szc[bz] += 16
            if t + NZ < NT:
                P.W("gpsimd", "vs", vs_done[t])
                P.W("gpsimd", "asc", asc_done[t])
                P.do("gpsimd", lambda E, t=t, bz=bz: E.dma_start(
                    out=xt[bz][:, :], in_=X[(t + NZ) * 128:(t + NZ + 1) * 128, :]).then_inc(
                    lzs[bz], 16) or None)
                lzc[bz] += 16
        z_sync_vs = P.cnt["vs"]
        z_szc = dict(szc)
        for nm, cnt in (("lz0", lzc[0]), ("lz1", lzc[1]), ("lz2", lzc[2]),
                        ("sz0", szc[0]), ("sz1", szc[1]), ("sz2", szc[2])):
            P.W("gpsimd", nm, cnt)
        quiesce()
        barrier()

    v_h = nc.dram_tensor("v_h", [128, NT * 260], BF16, kind="Internal")
    with contextlib.ExitStack() as es2:
        sb2 = lambda name, shp, dt: es2.enter_context(nc.sbuf_tensor(name, shp, dt))
        KTs = [[sb2(f"KT{s}_{i}", [128, S], BF16) for i in range(2)] for s in range(2)]
        QTs = [[sb2(f"QT{s}_{i}", [128, OWN], BF16) for i in range(2)] for s in range(2)]
        Vt = sb2("Vt", [128, NT * 260], BF16)
        vstg = [sb2(f"vstg{i}", [128, 260], BF16) for i in range(2)]
        wq_s = sb2("wq_s", [128, 8 * 768], BF16)
        zt_s = [sb2(f"zt_s{i}", [128, 8 * 128], BF16) for i in range(2)]
        PAB = [sb2(f"PAB{i}", [128, 1024], BF16) for i in range(5)]
        pairB2 = [sb2(f"pairB{s}", [128, 2], F32) for s in range(2)]
        krm = [sb2(f"krm{i}", [128, 256], BF16) for i in range(3)]
        qrm = [sb2(f"qrm{i}", [128, 256], BF16) for i in range(3)]
        rotb = sb2("rotb", [128, 256], BF16)
        tmpb = sb2("tmpb", [128, 256], BF16)
        sqg = [sb2(f"sqg{i}", [128, 512], BF16) for i in range(2)]
        normK2 = sb2("normK2", [128, 4 * NT], F32)
        normQ2 = sb2("normQ2", [128, 4 * ONT], F32)
        mk2 = sb2("mk2", [128, 4], F32)
        mq2 = sb2("mq2", [128, 4], F32)
        m1c = sb2("m1c", [4, 1], F32)
        m2c = sb2("m2c", [4, 1], F32)
        mmv = sb2("mmv", [4, 1], F32)
        negb = sb2("negb", [4, 1], F32)
        negbrow = sb2("negbrow", [1, 4], F32)
        expBias = sb2("expBias", [128, 4], F32)
        stgE = sb2("stgE", [64, 512], BF16)
        stgO = sb2("stgO", [64, 512], BF16)
        stgOn = sb2("stgOn", [64, 512], BF16)
        recEf = sb2("recEf", [1, 512], F32)
        recEb = sb2("recEb", [1, 512], BF16)
        recOf = sb2("recOf", [1, 512], F32)
        recOb = sb2("recOb", [1, 512], BF16)
        lnscr = sb2("lnscr", [1, 1024], F32)

        P.do("gpsimd", lambda E: E.memset(
            Vt.rearrange("p (t h x) -> p t h x", h=4, x=65)[:, :, :, 64:65], 1.0))
        for i in range(2):
            P.do("gpsimd", lambda E, i=i: E.memset(
                vstg[i].rearrange("p (h x) -> p h x", x=65)[:, :, 64:65], 1.0),
                inc=("gsc" if i == 1 else None))

        a_wq_guard = [0]
        vh_load_guard = [0]

        def gen_A(pp, psA, psT, direct_v):
            KT = KTs[pp % 2]
            QT = QTs[pp % 2]
            pairB = pairB2[pp % 2]
            qc, kc_, vc_ = 256 * pp, D + 256 * pp, 2 * D + 256 * pp
            P.W("sync", "ts", a_wq_guard[0])
            P.W("sync", "stz")
            for dst, srcc in ((0, kc_), (256, qc), (512, vc_)):
                P.do("sync", lambda E, dst=dst, srcc=srcc: E.dma_start(
                    out=wq_s.rearrange("p (i c) -> p i c", c=768)[:, :, dst:dst + 256],
                    in_=wq_h.rearrange("(i p) c -> p i c", p=128)[:, :, srcc:srcc + 256]),
                    inc="lds", n=16)
            lds_wq = P.cnt["lds"]
            P.do("sync", lambda E: E.dma_start(
                out=zt_s[0][:, :],
                in_=zT_h.rearrange("(i p) s -> p i s", p=128)[:, :, 0:128]),
                inc="lds", n=16)
            lds_zt_of = {0: P.cnt["lds"]}
            ts_mm = {}
            evac_done = {}
            last_evac = [0]
            rope_done = {}
            tp_ts = {}
            ptcopy = {}
            sq_done = {}
            nred = {}
            vdma = [P.cnt["gdma"]]

            def rope(buf, t):
                bv = buf
                src_o = bv.rearrange("p (h d) -> p h d", d=64)[:, :, 0:32] \
                    .rearrange("p h (i two) -> p h i two", two=2)
                rot_v = rotb.rearrange("p (h d) -> p h d", d=64)[:, :, 0:32] \
                    .rearrange("p h (i two) -> p h i two", two=2)
                P.do("vector", lambda E, s=src_o, r=rot_v: E.tensor_copy(
                    r[:, :, :, 0:1], s[:, :, :, 1:2]))
                P.do("vector", lambda E, s=src_o, r=rot_v: E.tensor_copy(
                    r[:, :, :, 1:2], s[:, :, :, 0:1]), inc="vs")
                P.selfw("vector", "vs")
                cosT = ropeCb.rearrange("p (tt c) -> p tt c", c=32)[:, t, :]
                sinT = ropeSb.rearrange("p (tt c) -> p tt c", c=32)[:, t, :]
                cosB = bass.AP(cosT.tensor, cosT.offset,
                               [list(cosT.ap[0]), [0, 4], list(cosT.ap[1])])
                sinB = bass.AP(sinT.tensor, sinT.offset,
                               [list(sinT.ap[0]), [0, 4], list(sinT.ap[1])])
                reg = bv.rearrange("p (h d) -> p h d", d=64)[:, :, 0:32]
                rreg = rotb.rearrange("p (h d) -> p h d", d=64)[:, :, 0:32]
                treg = tmpb.rearrange("p (h d) -> p h d", d=64)[:, :, 0:32]
                P.do("vector", lambda E, rr=rreg, s=sinB, tr=treg: E.tensor_tensor(
                    tr, rr, s, ALU.mult))
                P.do("vector", lambda E, rg=reg, c=cosB: E.tensor_tensor(
                    rg, rg, c, ALU.mult), inc="vs")
                P.selfw("vector", "vs")
                P.do("vector", lambda E, rg=reg, tr=treg: E.tensor_tensor(
                    rg, rg, tr, ALU.add), inc="vs")

            def emit_tp(tt):
                own_tt = tt < ONT
                P.W("tensor", "vs", rope_done[tt])
                if tt >= 1:
                    P.W("tensor", "vs", ptcopy.get(tt - 1, 0))
                P.do("tensor", lambda E, b3=tt % 3: E.transpose(
                    psT[:, 0:128], krm[b3][:, 0:128], identb[:, :]))
                P.do("tensor", lambda E, b3=tt % 3: E.transpose(
                    psT[:, 128:256], krm[b3][:, 128:256], identb[:, :]),
                    inc=(None if own_tt else "ts"))
                if own_tt:
                    P.do("tensor", lambda E, b3=tt % 3: E.transpose(
                        psT[:, 256:384], qrm[b3][:, 0:128], identb[:, :]))
                    P.do("tensor", lambda E, b3=tt % 3: E.transpose(
                        psT[:, 384:512], qrm[b3][:, 128:256], identb[:, :]), inc="ts")
                tp_ts[tt] = P.cnt["ts"]
                P.W("vector", "ts", tp_ts[tt])
                P.do("vector", lambda E, tt=tt, KT=KT: E.tensor_copy(
                    KT[0][:, tt * 128:(tt + 1) * 128], psT[:, 0:128]))
                P.do("vector", lambda E, tt=tt, KT=KT: E.tensor_copy(
                    KT[1][:, tt * 128:(tt + 1) * 128], psT[:, 128:256]),
                    inc=(None if own_tt else "vs"))
                if own_tt:
                    P.do("vector", lambda E, tt=tt, QT=QT: E.tensor_copy(
                        QT[0][:, tt * 128:(tt + 1) * 128], psT[:, 256:384]))
                    P.do("vector", lambda E, tt=tt, QT=QT: E.tensor_copy(
                        QT[1][:, tt * 128:(tt + 1) * 128], psT[:, 384:512]), inc="vs")
                ptcopy[tt] = P.cnt["vs"]

            for t in range(NT):
                b3 = t % 3
                bi = t % 2
                own = t < ONT
                if t + 1 < NT:
                    if t >= 1:
                        P.W("sync", "ts", ts_mm[t - 1])
                    P.do("sync", lambda E, t=t: E.dma_start(
                        out=zt_s[(t + 1) % 2][:, :],
                        in_=zT_h.rearrange("(i p) s -> p i s", p=128)[:, :, (t + 1) * 128:(t + 2) * 128]),
                        inc="lds", n=16)
                    lds_zt_of[t + 1] = P.cnt["lds"]
                if t >= 2:
                    emit_tp(t - 2)
                nfill = 2 if own else 1
                for fill in range(nfill):
                    P.W("tensor", "lds", max(lds_zt_of[t], lds_wq))
                    P.W("tensor", "vs", last_evac[0])
                    for i in range(8):
                        if i == 4:
                            yield
                        st_, sp_ = (i == 0), (i == 7)
                        if fill == 0 and own:
                            P.do("tensor", lambda E, bi=bi, i=i, st_=st_, sp_=sp_: E.matmul(
                                psA[:, 0:512], zt_s[bi][:, i * 128:(i + 1) * 128],
                                wq_s[:, i * 768:i * 768 + 512], start=st_, stop=sp_))
                        elif fill == 0:
                            P.do("tensor", lambda E, bi=bi, i=i, st_=st_, sp_=sp_: E.matmul(
                                psA[:, 0:256], zt_s[bi][:, i * 128:(i + 1) * 128],
                                wq_s[:, i * 768:i * 768 + 256], start=st_, stop=sp_))
                            P.do("tensor", lambda E, bi=bi, i=i, st_=st_, sp_=sp_: E.matmul(
                                psA[:, 256:512], zt_s[bi][:, i * 128:(i + 1) * 128],
                                wq_s[:, i * 768 + 512:i * 768 + 768], start=st_, stop=sp_))
                        else:
                            P.do("tensor", lambda E, bi=bi, i=i, st_=st_, sp_=sp_: E.matmul(
                                psA[:, 0:256], zt_s[bi][:, i * 128:(i + 1) * 128],
                                wq_s[:, i * 768 + 512:i * 768 + 768], start=st_, stop=sp_))
                    if fill == 0:
                        P.do("tensor", lambda E, kc_=kc_: E.matmul(
                            psA[:, 0:256], onesb[:, :], brow_b[:, kc_:kc_ + 256],
                            start=False, stop=True))
                        if own:
                            P.do("tensor", lambda E, qc=qc: E.matmul(
                                psA[:, 256:512], onesb[:, :], brow_b[:, qc:qc + 256],
                                start=False, stop=True), inc="ts")
                        else:
                            P.do("tensor", lambda E, vc_=vc_: E.matmul(
                                psA[:, 256:512], onesb[:, :], brow_b[:, vc_:vc_ + 256],
                                start=False, stop=True), inc="ts")
                    else:
                        P.do("tensor", lambda E, vc_=vc_: E.matmul(
                            psA[:, 0:256], onesb[:, :], brow_b[:, vc_:vc_ + 256],
                            start=False, stop=True), inc="ts")
                    ts_mm[t] = P.cnt["ts"]

                    P.W("vector", "ts", ts_mm[t])
                    if fill == 0:
                        if t >= 3:
                            P.W("vector", "ts", tp_ts.get(t - 3, 0))
                            P.W("vector", "gsc", sq_done.get(t - 3, 0))
                        P.do("vector", lambda E, b3=b3: E.tensor_copy(
                            krm[b3][:, :], psA[:, 0:256]), inc="vs")
                        if own:
                            P.do("vector", lambda E, b3=b3: E.tensor_copy(
                                qrm[b3][:, :], psA[:, 256:512]), inc="vs")
                        else:
                            if direct_v:
                                P.do("vector", lambda E, t=t: E.tensor_copy(
                                    Vt.rearrange("p (tt h x) -> p tt h x", h=4, x=65)[:, t, :, 0:64],
                                    psA[:, 256:512].rearrange("p (h x) -> p h x", x=64)), inc="vs")
                            else:
                                P.W("vector", "gdma", vdma[0])
                                P.do("vector", lambda E, t=t: E.tensor_copy(
                                    vstg[t % 2].rearrange("p (h x) -> p h x", x=65)[:, :, 0:64],
                                    psA[:, 256:512].rearrange("p (h x) -> p h x", x=64)), inc="vs")
                    else:
                        if direct_v:
                            P.do("vector", lambda E, t=t: E.tensor_copy(
                                Vt.rearrange("p (tt h x) -> p tt h x", h=4, x=65)[:, t, :, 0:64],
                                psA[:, 0:256].rearrange("p (h x) -> p h x", x=64)), inc="vs")
                        else:
                            P.W("vector", "gdma", vdma[0])
                            P.do("vector", lambda E, t=t: E.tensor_copy(
                                vstg[t % 2].rearrange("p (h x) -> p h x", x=65)[:, :, 0:64],
                                psA[:, 0:256].rearrange("p (h x) -> p h x", x=64)), inc="vs")
                    evac_done[(t, fill)] = P.cnt["vs"]
                    last_evac[0] = P.cnt["vs"]

                    if (fill == 0 and not own) or fill == 1:
                        if not direct_v:
                            P.W("gpsimd", "vs", evac_done[(t, fill)])
                            P.W("gpsimd", "lds", vh_load_guard[0])
                            P.do("gpsimd", lambda E, t=t: E.dma_start(
                                out=v_h[:, t * 260:(t + 1) * 260],
                                in_=vstg[t % 2][:, :]), inc="gdma", n=16)
                            vdma[0] = P.cnt["gdma"]
                    if fill == 0:
                        rope(krm[b3], t)
                        if own:
                            P.selfw("vector", "vs")
                            rope(qrm[b3], t)
                        rope_done[t] = P.cnt["vs"]
                        P.W("gpsimd", "vs", rope_done[t])
                        if t >= 2:
                            P.W("gpsimd", "vs", nred.get(t - 2, 0))
                        P.do("gpsimd", lambda E, b3=b3, t=t: E.tensor_tensor(
                            sqg[t % 2].rearrange("p (g h d) -> p g h d", g=2, d=64)[:, 0],
                            krm[b3].rearrange("p (h d) -> p h d", d=64),
                            krm[b3].rearrange("p (h d) -> p h d", d=64), ALU.mult),
                            inc=(None if own else "gsc"))
                        if own:
                            P.do("gpsimd", lambda E, b3=b3, t=t: E.tensor_tensor(
                                sqg[t % 2].rearrange("p (g h d) -> p g h d", g=2, d=64)[:, 1],
                                qrm[b3].rearrange("p (h d) -> p h d", d=64),
                                qrm[b3].rearrange("p (h d) -> p h d", d=64), ALU.mult), inc="gsc")
                        sq_done[t] = P.cnt["gsc"]
                        P.W("vector", "gsc", sq_done[t])
                        P.do("vector", lambda E, t=t: E.reduce_sum(
                            normK2.rearrange("p (h t) -> p h t", t=NT)[:, :, t:t + 1],
                            _ins1(sqg[t % 2].rearrange("p (g h d) -> p g h d", g=2, d=64)[:, 0], 2),
                            axis=AX.X), inc=(None if own else "vs"))
                        if own:
                            P.do("vector", lambda E, t=t: E.reduce_sum(
                                normQ2.rearrange("p (h t) -> p h t", t=ONT)[:, :, t:t + 1],
                                _ins1(sqg[t % 2].rearrange("p (g h d) -> p g h d", g=2, d=64)[:, 1], 2),
                                axis=AX.X), inc="vs")
                        nred[t] = P.cnt["vs"]
                    yield
            emit_tp(NT - 2)
            emit_tp(NT - 1)
            a_wq_guard[0] = P.cnt["ts"]

            psM = psA
            P.selfw("vector", "vs")
            P.do("vector", lambda E: E.reduce_max(
                _ins1(mk2[:, :], 2),
                _ins1(normK2.rearrange("p (h t) -> p h t", t=NT), 2), axis=AX.X), inc="vs")
            P.do("vector", lambda E: E.reduce_max(
                _ins1(mq2[:, :], 2),
                _ins1(normQ2.rearrange("p (h t) -> p h t", t=ONT), 2), axis=AX.X), inc="vs")
            vs_m = P.cnt["vs"]
            P.W("tensor", "vs", vs_m)
            P.do("tensor", lambda E: E.transpose(psM[0:4, 0:128], mk2[:, :], ident32[:, :]))
            P.do("tensor", lambda E: E.transpose(psM[0:4, 128:256], mq2[:, :], ident32[:, :]),
                 inc="ts")
            ts_tp = P.cnt["ts"]
            P.W("vector", "ts", ts_tp)
            P.do("vector", lambda E: E.reduce_max(
                m1c[:, :], _ins1(psM[0:4, 0:128], 1), axis=AX.X))
            P.do("vector", lambda E: E.reduce_max(
                m2c[:, :], _ins1(psM[0:4, 128:256], 1), axis=AX.X), inc="vs")
            P.selfw("vector", "vs")
            P.do("vector", lambda E: E.tensor_tensor(
                mmv[:, :], m1c[:, :], m2c[:, :], ALU.mult), inc="vs")
            vs_mm = P.cnt["vs"]
            P.W("scalar", "vs", vs_mm)
            P.do("scalar", lambda E: E.activation(
                negb[:, :], mmv[:, :], AF.Ln), inc="asc")
            P.selfw("scalar", "asc")
            P.do("scalar", lambda E: E.activation(
                negb[:, :], negb[:, :], AF.Exp, scale=0.5), inc="asc")
            P.selfw("scalar", "asc")
            P.do("scalar", lambda E: E.mul(negb[:, :], negb[:, :], -0.125), inc="asc")
            asc_nb = P.cnt["asc"]
            P.W("tensor", "asc", asc_nb)
            P.do("tensor", lambda E: E.transpose(
                psM[0:1, 256:260], negb[:, :], ident32[0:4, 0:4]), inc="ts")
            P.W("vector", "ts")
            P.do("vector", lambda E: E.tensor_copy(negbrow[:, :], psM[0:1, 256:260]), inc="vs")
            P.W("tensor", "vs")
            P.do("tensor", lambda E: E.matmul(
                psM[:, 384:388], ones32[:, :], negbrow[:, :], start=True, stop=True),
                inc="ts")
            P.W("vector", "ts")
            P.do("vector", lambda E: E.tensor_copy(expBias[:, :], psM[:, 384:388]), inc="vs")
            P.selfw("vector", "vs")
            P.do("vector", lambda E, pairB=pairB: E.tensor_tensor(
                _ins1(pairB[:, :], 2),
                expBias.rearrange("p (a b) -> p a b", b=2)[:, :, 0:1],
                expBias.rearrange("p (a b) -> p a b", b=2)[:, :, 1:2],
                ALU.min), inc="vs")
            yield

        with contextlib.ExitStack() as esA0:
            psA0 = es_psum(nc, esA0, "psA0", [128, 512])
            psT0 = esA0.enter_context(nc.psum_tensor("psT0", [128, 512], BF16))
            P.W("gpsimd", "gsc")
            g0 = gen_A(0, psA0, psT0, True)
            for _ in g0:
                pass
            quiesce()
            barrier()

        for p in range(NPASS):
            with contextlib.ExitStack() as esB:
                scb = [es_psum(nc, esB, f"scb{p}_{i}", [128, 1024]) for i in range(2)]
                ctxE = es_psum(nc, esB, f"ctxE{p}", [65, 512])
                ctxO = es_psum(nc, esB, f"ctxO{p}", [65, 512])
                if p < NPASS - 1:
                    psA1 = es_psum(nc, esB, f"psA1_{p}", [128, 512])
                    psT1 = esB.enter_context(nc.psum_tensor(f"psT1_{p}", [128, 512], BF16))
                    agen = gen_A(p + 1, psA1, psT1, False)
                else:
                    agen = None
                acount = [0]

                def a_step():
                    if agen is not None:
                        try:
                            next(agen)
                            acount[0] += 1
                        except StopIteration:
                            pass

                vt_ld = {}
                if p >= 1:
                    for part in range(4):
                        P.do("sync", lambda E, part=part: E.dma_start(
                            out=Vt[:, part * 16 * 260:(part + 1) * 16 * 260],
                            in_=v_h[:, part * 16 * 260:(part + 1) * 16 * 260]),
                            inc="lds", n=16)
                        vt_ld[part] = P.cnt["lds"]
                    vh_load_guard[0] = P.cnt["lds"]
                else:
                    for part in range(4):
                        vt_ld[part] = 0

                KTp = KTs[p % 2]
                QTp = QTs[p % 2]
                pairBp = pairB2[p % 2]

                prev_pv = [0]
                prev_exp = [0]
                pending_tail = [None]
                prev_tail_tt = [P.cnt["vs"]]
                sc_done = {}
                exp_done = {}
                pv_done = {}
                ggc = [0]
                ccount = [0]
                for hp in range(2):
                    KTx, QTx = KTp[hp], QTp[hp]
                    hA, hB = 2 * hp, 2 * hp + 1
                    c = 2 * p + hp
                    for qb in range(4):
                        gg0 = ggc[0]

                        def emit_pv_chunk(k, qb=qb, gg0=gg0):
                            gk = gg0 + k
                            P.W("tensor", "asc", exp_done[gk])
                            P.W("tensor", "lds", vt_ld[min(k // 16, 3)])
                            if k == 0:
                                P.W("tensor", "vs", prev_tail_tt[0])
                            for ctx, cb, hloc in ((ctxE, 0, hA), (ctxO, 512, hB)):
                                P.do("tensor", lambda E, k=k, gk=gk, ctx=ctx, cb=cb, hloc=hloc: E.matmul(
                                    ctx[:, :],
                                    Vt[:, k * 260 + hloc * 65: k * 260 + hloc * 65 + 65],
                                    PAB[gk % 5][:, cb:cb + 512],
                                    start=(k == 0), stop=(k == 63)),
                                    inc=("ts" if cb == 512 else None))
                            pv_done[gk] = P.cnt["ts"]

                        for g in range(NT):
                            gg = gg0 + g
                            if gg >= 2:
                                P.W("tensor", "asc", exp_done[gg - 2])
                            for KT_lo, cb in ((0, 0), (64, 512)):
                                P.do("tensor", lambda E, g=g, gg=gg, KT_lo=KT_lo, cb=cb, qb=qb,
                                     KTx=KTx, QTx=QTx: E.matmul(
                                    scb[gg % 2][:, cb:cb + 512],
                                    KTx[KT_lo:KT_lo + 64, g * 128:(g + 1) * 128],
                                    QTx[KT_lo:KT_lo + 64, qb * 512:(qb + 1) * 512],
                                    start=True, stop=True,
                                    tile_position=(KT_lo, 0)),
                                    inc=("ts" if KT_lo == 64 else None))
                            sc_done[gg] = P.cnt["ts"]
                            if g == 2 and pending_tail[0] is not None:
                                pending_tail[0]()
                                pending_tail[0] = None
                            if g >= 2:
                                emit_pv_chunk(g - 2)
                            P.W("scalar", "ts", sc_done[gg])
                            if gg >= 5:
                                P.W("scalar", "ts", pv_done[gg - 5])
                            P.do("scalar", lambda E, gg=gg, hp=hp, scb=scb, pairBp=pairBp: E.activation(
                                PAB[gg % 5][:, :], scb[gg % 2][:, :], AF.Exp,
                                bias=pairBp[:, hp:hp + 1], scale=0.125), inc="asc")
                            exp_done[gg] = P.cnt["asc"]
                            ccount[0] += 1
                            if ccount[0] % 3 == 0:
                                a_step()
                        emit_pv_chunk(NT - 2)
                        emit_pv_chunk(NT - 1)
                        ggc[0] += NT

                        ts_pv_last = P.cnt["ts"]
                        P.W("scalar", "ts", ts_pv_last)
                        P.do("scalar", lambda E, ctxE=ctxE: E.activation(
                            lnscr[:, 0:512], ctxE[64:65, :], AF.Ln))
                        P.do("scalar", lambda E, ctxO=ctxO: E.activation(
                            lnscr[:, 512:1024], ctxO[64:65, :], AF.Ln), inc="asc")
                        P.selfw("scalar", "asc")
                        P.do("scalar", lambda E: E.activation(
                            recEf[:, :], lnscr[:, 0:512], AF.Exp, scale=-1.0))
                        P.do("scalar", lambda E: E.activation(
                            recOf[:, :], lnscr[:, 512:1024], AF.Exp, scale=-1.0), inc="asc")
                        asc_rec = P.cnt["asc"]
                        P.W("vector", "asc", asc_rec)
                        P.W("vector", "ts", ts_pv_last)
                        P.do("vector", lambda E: E.tensor_copy(recEb[:, :], recEf[:, :]))
                        P.do("vector", lambda E: E.tensor_copy(recOb[:, :], recOf[:, :]))
                        P.do("vector", lambda E, ctxE=ctxE: E.tensor_copy(stgE[:, :], ctxE[0:64, :]))
                        if qb > 0 or hp > 0 or p > 0:
                            P.W("vector", "gdma")
                        P.do("vector", lambda E, ctxO=ctxO: E.tensor_copy(stgO[:, :], ctxO[0:64, :]), inc="vs")
                        vs_raw = P.cnt["vs"]

                        def make_tail(vs_raw=vs_raw, c=c, qb=qb, ctxE=ctxE, ctxO=ctxO):
                            def tail():
                                P.W("tensor", "vs", vs_raw)
                                P.do("tensor", lambda E, ctxE=ctxE: E.matmul(
                                    ctxE[0:64, :], onesb[0:1, 0:64], recEb[:, :],
                                    start=True, stop=True))
                                P.do("tensor", lambda E, ctxO=ctxO: E.matmul(
                                    ctxO[0:64, :], onesb[0:1, 0:64], recOb[:, :],
                                    start=True, stop=True), inc="ts")
                                ts_bc = P.cnt["ts"]
                                P.W("vector", "ts", ts_bc)
                                P.do("vector", lambda E, ctxE=ctxE: E.tensor_tensor(
                                    ctx_store[0:64, c * OWN + qb * 512: c * OWN + (qb + 1) * 512],
                                    stgE[:, :], ctxE[0:64, :], ALU.mult))
                                P.do("vector", lambda E, ctxO=ctxO: E.tensor_tensor(
                                    stgOn[:, :], stgO[:, :], ctxO[0:64, :], ALU.mult), inc="vs")
                                prev_tail_tt[0] = P.cnt["vs"]
                                P.W("gpsimd", "vs", prev_tail_tt[0])
                                P.do("gpsimd", lambda E: E.dma_start(
                                    out=ctx_store[64:128, c * OWN + qb * 512: c * OWN + (qb + 1) * 512],
                                    in_=stgOn[:, :]), inc="gdma", n=16)
                            return tail

                        pending_tail[0] = make_tail()

                if pending_tail[0] is not None:
                    pending_tail[0]()
                    pending_tail[0] = None
                if agen is not None:
                    for _ in agen:
                        pass
                quiesce()
                barrier()

    with contextlib.ExitStack() as es5:
        sb5 = lambda name, shp, dt: es5.enter_context(nc.sbuf_tensor(name, shp, dt))
        wout_sb = sb5("wout_sb", [128, 8 * D], BF16)
        wfT_sb = sb5("wfT_sb", [128, 8 * D], BF16)
        boutrow = sb5("boutrow", [1, D], BF16)
        xt2 = [sb5(f"xt2_{i}", [128, D], F32) for i in range(2)]
        r1 = [sb5(f"r1_{i}", [128, D], F32) for i in range(2)]
        r2 = [sb5(f"r2_{i}", [128, D], F32) for i in range(2)]
        y1 = [sb5(f"y1_{i}", [128, D], BF16) for i in range(2)]
        y1T = [sb5(f"y1T_{i}", [128, D], BF16) for i in range(2)]
        y2 = [sb5(f"y2_{i}", [128, D], F32) for i in range(2)]
        bns5 = [sb5(f"bns5_{i}", [128, 12], F32) for i in range(2)]
        st5 = [sb5(f"st5_{i}", [128, 2], F32) for i in range(2)]
        ln5 = [sb5(f"ln5_{i}", [128, 1], F32) for i in range(2)]
        rstd5 = [sb5(f"rstd5_{i}", [128, 1], F32) for i in range(2)]
        negm5 = [sb5(f"negm5_{i}", [128, 1], F32) for i in range(2)]
        ps_po = [es_psum(nc, es5, "ps_po0", [128, 1024])] * 2
        ps_ffn = [es_psum(nc, es5, f"ps_ffn{i}", [128, 1024]) for i in range(2)]
        ps_t5 = [es5.enter_context(nc.psum_tensor(f"ps_t5_{i}", [128, 1024], BF16)) for i in range(2)]

        P.do("sync", lambda E: E.dma_start(
            out=wout_sb.rearrange("p (c d) -> p c d", d=D),
            in_=wo_h.rearrange("(c p) d -> p c d", p=128)), inc="lds", n=16)
        P.do("sync", lambda E: E.dma_start(
            out=wfT_sb.rearrange("p (c d) -> p c d", d=D),
            in_=wf_h.rearrange("(c p) d -> p c d", p=128)), inc="lds", n=16)
        P.do("gpsimd", lambda E: E.dma_start(out=boutrow[:, :], in_=BoutRow[:, :]), inc="gdma", n=16)
        P.W("tensor", "gdma")
        lds_w5 = P.cnt["lds"]

        sub_state = {}
        gi = [0]
        for sbk in range(4):
            for sub in range(4):
                g = gi[0]; gi[0] += 1
                pb = g % 2
                t5 = sbk * 4 + sub
                P.W("sync", "vs")
                P.do("sync", lambda E, t5=t5, pb=pb: E.dma_start(
                    out=xt2[pb][:, :], in_=X[t5 * 128:(t5 + 1) * 128, :]), inc="lds", n=16)
                lds_x = P.cnt["lds"]

                P.W("tensor", "lds", lds_w5)
                if g >= 1:
                    P.W("tensor", "vs", sub_state[g - 1]["r1_add"])
                for dh in range(2):
                    for cch in range(8):
                        P.do("tensor", lambda E, t5=t5, cch=cch, dh=dh, pb=pb: E.matmul(
                            ps_po[pb][:, dh * 512:(dh + 1) * 512],
                            ctx_store[:, cch * OWN + t5 * 128: cch * OWN + (t5 + 1) * 128],
                            wout_sb[:, cch * D + dh * 512: cch * D + (dh + 1) * 512],
                            start=(cch == 0), stop=False))
                    P.do("tensor", lambda E, dh=dh, pb=pb: E.matmul(
                        ps_po[pb][:, dh * 512:(dh + 1) * 512],
                        onesb[:, :], boutrow[:, dh * 512:(dh + 1) * 512],
                        start=False, stop=True),
                        inc=("ts" if dh == 1 else None))
                ts_po = P.cnt["ts"]

                P.W("vector", "ts", ts_po)
                P.W("vector", "lds", lds_x)
                if g >= 2:
                    P.W("vector", "asc", sub_state[g - 2]["y1_apply"])
                P.do("vector", lambda E, pb=pb: E.scalar_tensor_tensor(
                    r1[pb][:, :], xt2[pb][:, :], 2.0, ps_po[pb][:, :],
                    ALU.mult, ALU.add), inc="vs")
                P.selfw("vector", "vs")

                def ln_block(srcb, pb):
                    P.do("vector", lambda E, srcb=srcb, pb=pb: E.bn_stats(
                        bns5[pb][:, 0:6], srcb[:, 0:512]))
                    P.do("vector", lambda E, srcb=srcb, pb=pb: E.bn_stats(
                        bns5[pb][:, 6:12], srcb[:, 512:1024]), inc="vs")
                    P.selfw("vector", "vs")
                    if g >= 2:
                        P.W("vector", "asc", sub_state[g - 2]["y2_apply"])
                    P.do("vector", lambda E, pb=pb: E.bn_aggr(st5[pb][:, :], bns5[pb][:, :]),
                         inc="vs")
                    vs_a = P.cnt["vs"]
                    P.W("scalar", "vs", vs_a)
                    P.do("scalar", lambda E, pb=pb: E.activation(
                        ln5[pb][:, :], st5[pb][:, 1:2], AF.Ln, bias=epst[:, 0:1]), inc="asc")
                    P.selfw("scalar", "asc")
                    P.do("scalar", lambda E, pb=pb: E.activation(
                        rstd5[pb][:, :], ln5[pb][:, :], AF.Exp, scale=-0.5), inc="asc")
                    asc_r = P.cnt["asc"]
                    P.W("vector", "asc", asc_r)
                    P.do("vector", lambda E, pb=pb: E.tensor_scalar(
                        negm5[pb][:, :], st5[pb][:, 0:1], rstd5[pb][:, 0:1], -1.0,
                        ALU.mult, ALU.mult), inc="vs")
                    return P.cnt["vs"]

                vs_n1 = ln_block(r1[pb], pb)
                P.W("scalar", "vs", vs_n1)
                if g >= 2:
                    P.W("scalar", "ts", sub_state[g - 2]["ffn_done"])
                P.do("scalar", lambda E, pb=pb: E.activation(
                    y1[pb][:, :], r1[pb][:, :], AF.Identity, bias=negm5[pb][:, 0:1],
                    scale=rstd5[pb][:, 0:1]), inc="asc")
                y1_apply = P.cnt["asc"]

                P.W("tensor", "asc", y1_apply)
                if g >= 2:
                    P.W("tensor", "vs", sub_state[g - 2]["y1T_evac"])
                for j in range(8):
                    P.do("tensor", lambda E, j=j, pb=pb: E.transpose(
                        ps_t5[pb][:, j * 128:(j + 1) * 128],
                        y1[pb][:, j * 128:(j + 1) * 128], identb[:, :]),
                        inc=("ts" if j == 7 else None))
                ts_t5 = P.cnt["ts"]
                P.W("vector", "ts", ts_t5)
                if g >= 2:
                    P.W("vector", "ts", sub_state[g - 2]["ffn_done"])
                P.do("vector", lambda E, pb=pb: E.tensor_copy(
                    y1T[pb][:, :], ps_t5[pb][:, :]), inc="vs")
                y1T_evac = P.cnt["vs"]

                P.W("tensor", "vs", y1T_evac)
                if g >= 2:
                    P.W("tensor", "vs", sub_state[g - 2]["r2_evac"])
                for ob in range(2):
                    for j in range(8):
                        P.do("tensor", lambda E, ob=ob, j=j, pb=pb: E.matmul(
                            ps_ffn[pb][:, ob * 512:(ob + 1) * 512],
                            y1T[pb][:, j * 128:(j + 1) * 128],
                            wfT_sb[:, j * D + ob * 512: j * D + (ob + 1) * 512],
                            start=(j == 0), stop=(j == 7)),
                            inc=("ts" if (ob == 1 and j == 7) else None))
                ffn_done = P.cnt["ts"]
                P.W("vector", "ts", ffn_done)
                P.do("vector", lambda E, pb=pb: E.tensor_tensor(
                    r2[pb][:, 0:512], ps_ffn[pb][:, 0:512], y1[pb][:, 0:512], ALU.add))
                P.do("vector", lambda E, pb=pb: E.tensor_tensor(
                    r2[pb][:, 512:1024], ps_ffn[pb][:, 512:1024], y1[pb][:, 512:1024],
                    ALU.add), inc="vs")
                r2_evac = P.cnt["vs"]
                P.selfw("vector", "vs")
                vs_n2 = ln_block(r2[pb], pb)
                P.W("scalar", "vs", vs_n2)
                if g >= 2:
                    P.W("scalar", "sto", 16 * (g - 1))
                P.do("scalar", lambda E, pb=pb: E.activation(
                    y2[pb][:, :], r2[pb][:, :], AF.Identity, bias=negm5[pb][:, 0:1],
                    scale=rstd5[pb][:, 0:1]), inc="asc")
                y2_apply = P.cnt["asc"]
                P.do("scalar", lambda E, t5=t5, pb=pb: E.dma_start(
                    out=OUT[t5 * 128:(t5 + 1) * 128, :], in_=y2[pb][:, :]), inc="sto", n=16)
                sub_state[g] = dict(y1_apply=y1_apply, y1T_evac=y1T_evac,
                                    ffn_done=ffn_done, r2_evac=r2_evac,
                                    y2_apply=y2_apply, r1_add=P.cnt["vs"])

        P.W("sync", "sto")

    with nc.Block() as block:
        @block.sync
        def _(sync):
            for fn in P.ops["sync"]:
                fn(sync)

        @block.gpsimd
        def _(gpsimd):
            for fn in P.ops["gpsimd"]:
                fn(gpsimd)

        @block.vector
        def _(vector):
            for fn in P.ops["vector"]:
                fn(vector)

        @block.tensor
        def _(tensor):
            for fn in P.ops["tensor"]:
                fn(tensor)

        @block.scalar
        def _(scalar):
            for fn in P.ops["scalar"]:
                fn(scalar)

    es.close()
    return nc


def es_psum(nc, es, name, shp):
    return es.enter_context(nc.psum_tensor(name, shp, F32))


def _rope_tables():
    RD = 32
    inv = (1.0 / 10000.0 ** (np.arange(0, RD, 2, dtype=np.float32) / np.float32(RD))).astype(np.float32)
    t = np.arange(S, dtype=np.float32)
    ang = (t[:, None] * inv[None, :]).astype(np.float32)
    fr = np.repeat(ang, 2, axis=1).astype(np.float64)
    cos = np.cos(fr).astype(np.float32)
    sin = np.sin(fr)
    sgn = np.ones((1, RD))
    sgn[0, 0::2] = -1.0
    sins = (sin * sgn).astype(np.float32)
    return cos, sins


_NC_CACHE = {}


def kernel(L0_attn_X, L0_attn_W_in, L0_attn_b_in, L0_attn_W_out, L0_attn_b_out,
           L0_attn_gamma, L0_attn_beta, L0_ffn_weight):
    Xf = np.ascontiguousarray(np.asarray(L0_attn_X, dtype=np.float32))
    Win = np.ascontiguousarray(np.asarray(L0_attn_W_in, dtype=np.float32))
    bin_ = np.ascontiguousarray(np.asarray(L0_attn_b_in, dtype=np.float32).reshape(1, 3 * D))
    Wout = np.ascontiguousarray(np.asarray(L0_attn_W_out, dtype=np.float32))
    bout = np.ascontiguousarray(np.asarray(L0_attn_b_out, dtype=np.float32).reshape(8, 128).T)
    gam = np.ascontiguousarray(np.asarray(L0_attn_gamma, dtype=np.float32).reshape(8, 128).T)
    bet = np.ascontiguousarray(np.asarray(L0_attn_beta, dtype=np.float32).reshape(8, 128).T)
    Wffn = np.ascontiguousarray(np.asarray(L0_ffn_weight, dtype=np.float32))
    cos, sins = _rope_tables()

    if "nc" not in _NC_CACHE:
        _NC_CACHE["nc"] = _build()
    nc = _NC_CACHE["nc"]

    in_maps = []
    for core in range(8):
        b, q = core // 4, core % 4
        Xb = Xf[b]
        perm = np.concatenate([np.arange(q * OWN, (q + 1) * OWN),
                               np.arange(0, q * OWN),
                               np.arange((q + 1) * OWN, S)])
        in_maps.append({
            "X": np.ascontiguousarray(Xb[perm]),
            "BoutRow": np.ascontiguousarray(np.asarray(L0_attn_b_out, dtype=np.float32).reshape(1, D)),
            "Win": Win, "BinRow": bin_, "Wout": Wout, "BoutCol": bout,
            "GamCol": gam, "BetCol": bet, "Wffn": Wffn,
            "RopeC": np.ascontiguousarray(cos[perm]),
            "RopeS": np.ascontiguousarray(sins[perm]),
        })
    res = run_bass_kernel_spmd(nc, in_maps, core_ids=list(range(8)))
    out = np.empty((2, S, D), dtype=np.float32)
    for core in range(8):
        b, q = core // 4, core % 4
        out[b, q * OWN:(q + 1) * OWN] = res.results[core]["OUT"]
    return out
